# version 18
# speedup vs baseline: 1.1525x; 1.0120x over previous
import numpy as np
import concourse.bass as bass
import concourse.mybir as mybir
from concourse.bass_utils import run_bass_kernel_spmd

F32 = mybir.dt.float32
BF16 = mybir.dt.bfloat16
F16 = mybir.dt.float16
AF = mybir.ActivationFunctionType
ALU = mybir.AluOpType
AX = mybir.AxisListType

NEG = -1.0e30


def splits(total, step):
    out = []
    o = 0
    while o < total:
        out.append((o, min(step, total - o)))
        o += step
    return out


class PB:

    ENGS = ("pe", "dv", "ac", "gp", "sp")

    def __init__(self, nc):
        self.nc = nc
        self.steps = {e: [] for e in self.ENGS}
        self.sem = {}
        self.cnt = {}
        for e in self.ENGS:
            self.sem[e] = nc.alloc_semaphore(f"s_{e}")
            self.cnt[e] = 0
        for q in ("q_x", "q_wv0", "q_wv1", "q_wk", "q_wg0a", "q_wg0b",
                  "q_wm0", "q_wm1", "q_wm2", "q_wm3",
                  "q_wg1a", "q_wg1b", "q_w20", "q_w21", "q_w22", "q_w23", "q_w24",
                  "q_misc", "q_gw", "q_pst"):
            self.sem[q] = nc.alloc_semaphore(f"s_{q}")
            self.cnt[q] = 0
        self.waited = {e: {} for e in self.ENGS}
        self.selfser = {"dv", "ac", "gp"}
        self.bguard = {}

    def _emit(self, eng, deps, fn, inc_key, inc_amt):
        deps = [d for d in deps if d is not None]
        if eng in self.selfser and self.cnt[eng] > 0:
            deps = deps + [(eng, self.cnt[eng])]
        if inc_key is not None:
            self.cnt[inc_key] += inc_amt
        val_after = self.cnt[inc_key] if inc_key is not None else None
        w = self.waited[eng]
        needed = {}
        for (k, v) in deps:
            if v > w.get(k, 0):
                needed[k] = max(needed.get(k, 0), v)
        for k, v in needed.items():
            w[k] = v
        self.steps[eng].append((sorted(needed.items()), fn, inc_key, inc_amt))
        return (inc_key, val_after) if inc_key is not None else None

    def op(self, eng, deps, fn):
        return self._emit(eng, deps, fn, eng, 1)

    def dma(self, eng, queue, deps, fn):
        return self._emit(eng, deps, fn, queue, 16)

    def bank_dep(self, *banks):
        return [self.bguard.get(b) for b in banks]

    def bank_done(self, tok, *banks):
        for b in banks:
            self.bguard[b] = tok

    def realize(self, block):
        sems = self.sem
        used_pe = set()
        for e in self.ENGS:
            for (needed, _f, _k, _a) in self.steps[e]:
                for (k, v) in needed:
                    if k == "pe":
                        used_pe.add(v)

        def body(engobj, eng):
            pe_val = 0
            pending = 0
            n = len(self.steps[eng])
            for idx, (needed, fn, inc_key, inc_amt) in enumerate(self.steps[eng]):
                for (k, v) in needed:
                    engobj.wait_ge(sems[k], v)
                ins = fn(engobj)
                if inc_key == "pe":
                    pe_val += inc_amt
                    pending += inc_amt
                    if pe_val in used_pe or idx == n - 1:
                        ins.then_inc(sems["pe"], pending)
                        pending = 0
                elif inc_key is not None:
                    ins.then_inc(sems[inc_key], inc_amt)

        block.tensor(lambda t: body(t, "pe"))
        block.vector(lambda v: body(v, "dv"))
        block.scalar(lambda s: body(s, "ac"))
        block.gpsimd(lambda g: body(g, "gp"))
        block.sync(lambda s: body(s, "sp"))


def build_nc(S=2048, D=1024, F=3584, E=8, CAP=640, dbg=False):
    NT = S // 128
    MC = D // 128
    FC = F // 128
    FG = F // 512
    CC = (CAP + 127) // 128
    CH = CAP // 2
    LNC = splits(D, 512)
    MH = splits(D, 512)
    CS = splits(CAP, 512)
    SH = 2 if NT % 2 == 0 else 1
    ISQ = float(1.0 / np.sqrt(np.float32(D)))
    assert CH <= 512 and F % 512 == 0 and S % 512 == 0 and D % 128 == 0

    nc = bass.Bass("TRN2", target_bir_lowering=False)

    x_in = nc.dram_tensor("x", [S, D], F32, kind="ExternalInput")
    c_in = nc.dram_tensor("c", [1, D], F32, kind="ExternalInput")
    wmod_in = nc.dram_tensor("wmod", [D, 6 * D], F32, kind="ExternalInput")
    wk_in = nc.dram_tensor("wk", [D, D], F32, kind="ExternalInput")
    wq_in = nc.dram_tensor("wq", [D, D], F32, kind="ExternalInput")
    wv_in = nc.dram_tensor("wv", [D, D], F32, kind="ExternalInput")
    gw_in = nc.dram_tensor("gw", [E, D], F32, kind="ExternalInput")
    w1_in = nc.dram_tensor("w1", [E, D, F], F32, kind="ExternalInput")
    w2_in = nc.dram_tensor("w2", [E, F, D], F32, kind="ExternalInput")
    w3_in = nc.dram_tensor("w3", [E, D, F], F32, kind="ExternalInput")
    out_d = nc.dram_tensor("out", [S, D], F32, kind="ExternalOutput")
    x1_d = nc.dram_tensor("x1spill", [S, D], F32, kind="Internal")
    dbg_o = nc.dram_tensor("dbg_o", [S, D], F32, kind="Internal") if dbg else None
    dbg_P = nc.dram_tensor("dbg_P", [128, S], F32, kind="Internal") if dbg else None
    dbg_V = nc.dram_tensor("dbg_V", [128, S * D // 128], mybir.dt.bfloat16, kind="Internal") if dbg else None
    dbg_q = nc.dram_tensor("dbg_q", [128, D], mybir.dt.bfloat16, kind="Internal") if dbg else None
    dbg_nm = nc.dram_tensor("dbg_nm", [128, 2], F32, kind="Internal") if dbg else None
    dbg_tok = [None]

    ARENA = 212832
    base = nc.bump_sbuf(ARENA)[0]
    alloc_at = nc.alloc_sbuf_tensor_at
    off = [base]

    def place(sz):
        o = off[0]
        off[0] += (sz + 31) & ~31
        return o

    r0 = place(max(NT * D * 4, 2 * MC * S * 2, 2 * 2048 * 4))
    x1 = alloc_at("x1", [128, NT * D], F32, offset=r0)
    hT = alloc_at("hT", [128, MC * S], BF16, offset=r0)
    KT = alloc_at("KT", [128, MC * S], BF16, offset=r0 + MC * S * 2)
    NSLOT = 4
    wmod_b = alloc_at("wmod_b", [128, NSLOT * 1024], BF16,
                      offset=r0 + NT * D * 4 - NSLOT * 1024 * 2)

    r1 = place(max(NT * D * 2, FC * CAP * 2,
                   CC * (S // SH) * 2 + S * 2 + S * 4, 4 * D * 4))
    V = alloc_at("V", [128, NT * D], BF16, offset=r1)
    gT = alloc_at("gT", [128, FC * CAP], BF16, offset=r1)
    CM = alloc_at("CM", [128, CC * (S // SH)], BF16, offset=r1)

    DHALF = min(512, D)
    NDH = D // DHALF
    atn2 = MC * D * 2 + 2 * MC * DHALF * 2
    r2 = place(max(atn2, 2 * 2 * MC * 512 * 2))
    Wqb = alloc_at("Wqb", [128, MC * D], BF16, offset=r2)
    Wkv = [alloc_at(f"Wkv{h}", [128, MC * DHALF], BF16,
                    offset=r2 + MC * D * 2 + h * MC * DHALF * 2)
           for h in range(2)]
    Pts_hi = [alloc_at(f"Pt{2 + k}", [128, S], BF16,
                       offset=r2 + MC * D * 2 + k * S * 2) for k in range(2)]
    assert 2 * MC * DHALF * 2 >= 2 * S * 2
    GB = MC * 512 * 2
    wgbuf = []
    for i in range(2):
        wgbuf.append((alloc_at(f"w1g{i}", [128, MC * 512], BF16,
                               offset=r2 + i * 2 * GB),
                      alloc_at(f"w3g{i}", [128, MC * 512], BF16,
                               offset=r2 + i * 2 * GB + GB)))

    atn3 = 2 * D * 4 + D * 4 + S * 4 + 4 * D * 2 + NT * 128 * 2
    r3 = place(max(NT * D * 2, atn3))
    h2b = alloc_at("h2b", [128, NT * D], BF16, offset=r3)
    xin = alloc_at("xin", [128, 2 * D], F32, offset=r3)
    o_sb = alloc_at("o_sb", [128, D], F32, offset=r3 + 2 * D * 4)
    Pts = [alloc_at(f"Pt{k}", [128, S], BF16,
                    offset=r3 + 3 * D * 4 + k * S * 2) for k in range(2)] + Pts_hi
    hs1 = alloc_at("hs1", [128, D], F32, offset=r3 + 3 * D * 4)
    qt4 = alloc_at("qt4", [128, 4 * D], BF16, offset=r3 + 3 * D * 4 + S * 4)
    mod_stage = alloc_at("mod_stage", [1, 512], F32, offset=r3 + 3 * D * 4 + S * 4)
    cl_scr = alloc_at("cl_scr", [128, D], F32,
                      offset=r3 + 3 * D * 4 + S * 4 + 2048)
    PT = alloc_at("PT", [128, NT * 128], BF16,
                  offset=r3 + 3 * D * 4 + S * 4 + 4 * D * 2)

    s1win = max(max(CAP, 512) * 2, CAP * 2)
    r4 = place(NT * CH * 2 + s1win)
    De = alloc_at("De", [128, NT * CH], BF16, offset=r4)
    w2s = alloc_at("w2s", [128, MC * 512], BF16, offset=r4)
    s1 = alloc_at("s1", [128, max(CAP, 512)], BF16, offset=r4 + NT * CH * 2)
    prob_row = alloc_at("prob_row", [1, CAP], BF16, offset=r4 + NT * CH * 2)
    assert FC * CAP * 2 >= CC * (S // SH) * 2 + S * 2
    mkT_f = alloc_at("mkT_f", [E, S], F32, offset=r1 + CC * (S // SH) * 2 + S * 2)
    gw_s = alloc_at("gw_s", [E, D], F32, offset=r4)
    WkF = alloc_at("WkF", [128, MC * D], BF16, offset=r4)
    h2f_scr = [alloc_at(f"h2f_scr{k}", [128, D], F32, offset=r4 + k * 2 * D * 4)
               for k in range(2)]
    h2T_scr = [alloc_at(f"h2T_scr{k}", [128, D], F32,
                        offset=r4 + k * 2 * D * 4 + D * 4) for k in range(2)]

    r5 = place(max(MC * CAP * 2, CC * D * 2))
    disp = alloc_at("disp", [128, MC * CAP], BF16, offset=r5)
    out_e = alloc_at("out_e", [128, CC * D], BF16, offset=r5)

    rM = place(6 * D * 2)
    Mb = alloc_at("Mb", [128, 6 * D], BF16, offset=rM)

    def mslot(i):
        return Mb[:, i * D:(i + 1) * D]

    ones_f = alloc_at("ones_f", [1, 128], F32, offset=place(128 * 4))
    iotaP = alloc_at("iotaP", [8, 128], F32, offset=place(128 * 4))
    sel = alloc_at("sel", [8, 128], F16, offset=place(128 * 2))
    ones_b = alloc_at("ones_b", [1, 128], BF16, offset=place(128 * 2))
    posT = alloc_at("posT", [E, S], F16, offset=place(S * 2))

    rR = place(4 * NT * E * 4)
    probs_a = alloc_at("probs_a", [128, NT * E], F32, offset=rR)
    mk_a = alloc_at("mk_a", [128, NT * E], F32, offset=rR + NT * E * 4)
    C_a = alloc_at("C_a", [128, NT * E], F32, offset=rR + 2 * NT * E * 4)
    pos_a = alloc_at("pos_a", [128, NT * E], F32, offset=rR + 3 * NT * E * 4)
    lg_a = C_a

    identf = alloc_at("identf", [128, 128], F32, offset=place(128 * 4))
    identb = alloc_at("identb", [128, 128], BF16, offset=place(128 * 2))
    iw = max(CH, 128)
    iotac = alloc_at("iotac", [128, iw], F32, offset=place(iw * 4))
    pcol = alloc_at("pcol", [128, 1], F32, offset=place(4))
    ccol = alloc_at("ccol", [128, CC], F32, offset=place(CC * 4))
    strictU = alloc_at("strictU", [128, 128], F32, offset=place(128 * 4))
    gwT = alloc_at("gwT", [128, MC * E], F32, offset=place(MC * E * 4))
    ct = alloc_at("ct", [128, MC], F32, offset=place(MC * 4))
    cts = alloc_at("cts", [128, MC], BF16, offset=place(MC * 2))
    scr = alloc_at("scr", [128, 56], F32, offset=place(56 * 4))
    probs_b = alloc_at("probs_b", [128, NT * E], BF16, offset=place(NT * E * 2))
    bn6 = alloc_at("bn6", [128, 12 * len(LNC)], F32, offset=place(12 * len(LNC) * 4))
    st2 = alloc_at("st2", [128, 4], F32, offset=place(4 * 4))
    assert off[0] - base <= ARENA, f"SBUF over budget: {off[0] - base}"

    pb = [nc.alloc_psum_tensor(f"pb{i}", [128, 512], F32) for i in range(8)]
    pbb = [t.bitcast(BF16) for t in pb]

    p = PB(nc)

    def g_iota(ap, pattern, base, chm):
        return lambda g: g.iota(ap, pattern=pattern, base=base,
                                channel_multiplier=chm,
                                allow_small_or_imprecise_dtypes=True)

    t_iotac = p.op("gp", [], g_iota(iotac[:], [[1, iw]], 0, 0))
    t_pcol = p.op("gp", [], g_iota(pcol[:], [[0, 1]], 0, 1))
    t_ccol = p.op("gp", [], g_iota(ccol[:], [[128, CC]], 0, 1))
    t_ones = p.op("gp", [], lambda g: g.memset(ones_f[:], 1.0))
    t_iotaP = p.op("gp", [], g_iota(iotaP[:], [[0, 128]], 0, 1))
    t_onesb = p.op("gp", [], lambda g: g.memset(ones_b[:], 1.0))
    t_eps = p.op("gp", [], lambda g: g.memset(scr[:, 0:1], 1e-5))
    t_ct = p.dma("gp", "q_misc", [], lambda g: g.dma_start(
        ct[:, :], c_in.ap().rearrange("o (j q) -> o q j", q=128)[0],
        allow_slow_non_contiguous=True))
    t_gw = p.dma("gp", "q_gw", [], lambda g: g.dma_start(gw_s[:, :], gw_in.ap()))

    t_idf = p.op("dv", [t_iotac, t_pcol], lambda v: v.tensor_scalar(
        identf[:], iotac[:, 0:128], pcol[:], None, ALU.is_equal))
    t_idb = p.op("dv", [t_idf], lambda v: v.tensor_copy(identb[:], identf[:]))
    t_su = p.op("dv", [t_iotac, t_pcol], lambda v: v.tensor_scalar(
        strictU[:], iotac[:, 0:128], pcol[:], None, ALU.is_gt))
    t_dm = t_su

    t_gwT = []
    for j in range(MC):
        tt = p.op("pe", [t_gw, t_idf] + p.bank_dep(0),
                  lambda t, j=j: t.transpose(
                      pb[0][:, 0:E], gw_s[:, j * 128:(j + 1) * 128],
                      identf[0:E, 0:E]))
        te = p.op("dv", [tt], lambda v, j=j: v.tensor_copy(
            gwT[:, j * E:(j + 1) * E], pb[0][:, 0:E]))
        p.bank_done(te, 0)
        t_gwT.append(te)

    def wdma_half(dst, sem, src, half, deps):
        return p.dma("gp", sem, deps, lambda g: g.dma_start(
            dst[:], src.ap()[:, half * DHALF:(half + 1) * DHALF]
            .rearrange("(j q) d -> q j d", q=128)))

    wv_tok = [wdma_half(Wkv[h], f"q_wv{h}", wv_in, h, []) for h in range(NDH)]

    PIECE_SLOT = {0: 1, 1: 0, 2: 2, 3: 4, 4: 3, 5: 5}
    t_cts = p.op("ac", [t_ct, t_gw], lambda s: s.activation(cts[:], ct[:], AF.Silu))
    SUB = min(512, D)
    WB = 2 * SUB
    NCC = (6 * D) // WB
    pieces_done = {pc: [] for pc in range(6)}
    t_clamp = []
    wm_pe = {}
    mst = {"bc": None}

    def emit_clamp(pc):
        r_ = mslot(pc)
        sg = cl_scr[:]
        a1 = p.op("dv", pieces_done[pc] + t_clamp,
                  lambda v, r_=r_, sg=sg: v.tensor_scalar(
                      sg, r_, 0.0, 2.0, ALU.is_ge, ALU.mult))
        a2 = p.op("dv", [a1], lambda v, sg=sg: v.tensor_scalar(
            sg, sg, 1.0, None, ALU.subtract))
        a3 = p.op("dv", [a2], lambda v, r_=r_, sg=sg: v.tensor_tensor(
            r_, r_, sg, ALU.mult))
        a4 = p.op("dv", [a3], lambda v, r_=r_: v.tensor_scalar(
            r_, r_, 0.25, None, ALU.max))
        a5 = p.op("dv", [a4], lambda v, r_=r_, sg=sg: v.tensor_tensor(
            r_, r_, sg, ALU.mult))
        t_clamp.append(a5)

    def emit_piece_bcast(cc3, mmlast):
        for n in range(2):
            colg = cc3 * WB + n * SUB
            piece, incol = colg // D, colg % D
            mslot_i = PIECE_SLOT[piece]
            plus1 = piece in (1, 4)
            ev = p.op("dv", [mmlast, mst["bc"]], lambda v, n=n: v.tensor_copy(
                mod_stage[0:1, 0:SUB], pb[4 + n][0:1, 0:SUB]))
            p.bank_done(ev, 4 + n)
            mm = p.op("pe", [ev, t_ones] + p.bank_dep(6),
                      lambda t: t.matmul(pb[6][:, 0:SUB], ones_f[:],
                                         mod_stage[0:1, 0:SUB],
                                         start=True, stop=True))
            mst["bc"] = mm
            if plus1:
                e2 = p.op("dv", [mm], lambda v, mslot_i=mslot_i, incol=incol:
                          v.tensor_scalar(
                              Mb[:, mslot_i * D + incol:mslot_i * D + incol + SUB],
                              pb[6][:, 0:SUB], 1.0, None, ALU.add))
            else:
                e2 = p.op("dv", [mm], lambda v, mslot_i=mslot_i, incol=incol:
                          v.tensor_copy(
                              Mb[:, mslot_i * D + incol:mslot_i * D + incol + SUB],
                              pb[6][:, 0:SUB]))
            p.bank_done(e2, 6)
            pieces_done[piece].append(e2)
        if cc3 in (2, 5):
            emit_clamp(cc3)

    def emit_mod_pair(cc3, j):
        idx = cc3 * MC + j
        slot = idx % NSLOT
        tw = p.dma("gp", f"q_wm{slot}", [wm_pe.get(idx - NSLOT)],
                   lambda g, cc3=cc3, j=j, slot=slot: g.dma_start(
                       wmod_b[:, slot * WB:slot * WB + WB],
                       wmod_in.ap()[j * 128:(j + 1) * 128,
                                    cc3 * WB:(cc3 + 1) * WB]))
        last = None
        for n in range(2):
            bdep = p.bank_dep(4 + n) if j == 0 else []
            last = p.op("pe", [tw, t_cts] + bdep,
                        lambda t, j=j, n=n, slot=slot: t.matmul(
                            pb[4 + n][0:1, 0:SUB], cts[:, j:j + 1],
                            wmod_b[:, slot * WB + n * SUB:
                                   slot * WB + (n + 1) * SUB],
                            start=(j == 0), stop=(j == MC - 1)))
        wm_pe[idx] = last
        if j == MC - 1:
            emit_piece_bcast(cc3, last)

    MOD_ILV = NCC
    for cc3 in range(MOD_ILV):
        for j in range(MC):
            emit_mod_pair(cc3, j)
    mod_rest = [(cc3, j) for cc3 in range(MOD_ILV, NCC) for j in range(MC)]

    wq_toks = [p.dma("gp", "q_wk", [], lambda g: g.dma_start(
        Wqb[:].rearrange("q (j d) -> q j d", j=MC),
        wq_in.ap().rearrange("(j q) d -> q j d", q=128)))]
    t_wkF = p.dma("gp", "q_wg0b", t_gwT, lambda g: g.dma_start(
        WkF[:].rearrange("q (j d) -> q j d", j=MC),
        wk_in.ap().rearrange("(j q) d -> q j d", q=128)))

    modA = pieces_done[0] + pieces_done[1]

    def ln_stats(xt_ap, dep, war_dep, par=0):
        o = 8 * par
        nl = len(LNC)
        tb = []
        for k, (c0, cw) in enumerate(LNC):
            tb.append(p.op("dv", [dep] + ([war_dep] if k == 0 else []),
                           lambda v, k=k, c0=c0, cw=cw: v.bn_stats(
                               bn6[:, (par * nl + k) * 6:(par * nl + k + 1) * 6],
                               xt_ap[:, c0:c0 + cw])))
        ta = p.op("dv", tb, lambda v: v.bn_aggr(
            st2[:, 2 * par:2 * par + 2],
            bn6[:, par * nl * 6:(par + 1) * nl * 6]))
        s1_ = p.op("ac", [ta, t_eps], lambda s: s.activation(
            scr[:, 1 + o:2 + o], st2[:, 2 * par + 1:2 * par + 2], AF.Sqrt,
            bias=scr[:, 0:1]))
        s2_ = p.op("dv", [s1_], lambda v: v.reciprocal(
            scr[:, 2 + o:3 + o], scr[:, 1 + o:2 + o]))
        s3_ = p.op("dv", [s2_], lambda v: v.tensor_scalar(
            scr[:, 3 + o:4 + o], st2[:, 2 * par:2 * par + 1],
            scr[:, 2 + o:3 + o], -1.0, ALU.mult, ALU.mult))
        return s3_, s1_

    xin_tok = [None] * NT
    xin_free = {}
    v_pe_last = None
    v_last = None
    hT_last = [None] * NT
    prev_ts1 = None
    hscr = [o_sb[:], hs1[:]]
    th_tok = [None] * NT
    pend_evac = None

    def emit_a1_evacs(i):
        par = i % 2
        hs = hscr[par]
        last = None
        for jg in range(0, MC, 4):
            ng = min(4, MC - jg)
            bank = 2 * par + (jg // 4) % 2
            tt = None
            for j_i in range(ng):
                tt = p.op("pe", [th_tok[i], t_idf] +
                          (p.bank_dep(bank) if j_i == 0 else []),
                          lambda t, j=jg + j_i, j_i=j_i, bank=bank, hs=hs:
                          t.transpose(
                              pb[bank][:, j_i * 128:(j_i + 1) * 128],
                              hs[:, j * 128:(j + 1) * 128], identf[:]))
            ev = p.op("ac", [tt], lambda v, jg=jg, ng=ng, i=i, bank=bank:
                      v.copy(
                          hT[:].rearrange("q (j s) -> q j s", j=MC)
                          [:, jg:jg + ng, i * 128:(i + 1) * 128],
                          pb[bank][:, 0:ng * 128]
                          .rearrange("q (j s) -> q j s", j=ng)))
            p.bank_done(ev, bank)
            hT_tr_done[par] = tt
            last = ev
        hT_last[i] = last
        for half in range(NDH):
            bank = 2 * par + half
            vmm = None
            for j in range(MC):
                vmm = p.op("pe", [wv_tok[half], last] +
                           (p.bank_dep(bank) if j == 0 else []),
                           lambda t, j=j, i=i, half=half, bank=bank: t.matmul(
                               pb[bank][:, 0:DHALF],
                               hT[:, j * S + i * 128:j * S + (i + 1) * 128],
                               Wkv[half][:, j * DHALF:(j + 1) * DHALF],
                               start=(j == 0), stop=(j == MC - 1)))
            ev = p.op("ac", [vmm], lambda v, i=i, half=half, bank=bank:
                      v.copy(
                          V[:, i * D + half * DHALF:i * D + (half + 1) * DHALF],
                          pb[bank][:, 0:DHALF]))
            p.bank_done(ev, bank)
            return_vals[0] = vmm
            return_vals[1] = ev

    return_vals = [None, None]
    hT_tr_done = [None, None]
    for i in range(NT):
        slot = i % 2
        par = i % 2
        hs = hscr[par]
        o = 8 * par
        dep = [xin_free.get(slot), xin_tok[i - 1] if i else None]
        td = p.dma("sp", "q_x", dep, lambda s, i=i: s.dma_start(
            xin[:, (i % 2) * D:((i % 2) + 1) * D],
            x_in.ap()[i * 128:(i + 1) * 128, :]))
        xin_tok[i] = td
        xt = xin[:, slot * D:(slot + 1) * D]
        s3_, prev_ts1 = ln_stats(xt, xin_tok[i], prev_ts1, par)
        tl = p.op("dv", [s3_], lambda v, xt=xt, hs=hs, o=o: v.tensor_scalar(
            hs, xt, scr[:, 2 + o:3 + o], scr[:, 3 + o:4 + o],
            ALU.mult, ALU.add))
        tm = p.op("dv", [tl] + modA, lambda v, hs=hs: v.tensor_tensor(
            hs, hs, mslot(0), ALU.mult))
        th = p.op("dv", [tm, hT_tr_done[par]], lambda v, hs=hs: v.tensor_tensor(
            hs, hs, mslot(1), ALU.add))
        th_tok[i] = th
        xin_free[slot] = th
        if pend_evac is not None:
            emit_a1_evacs(pend_evac)
        pend_evac = i
        for _ in range(2):
            if mod_rest:
                emit_mod_pair(*mod_rest.pop(0))
    emit_a1_evacs(pend_evac)
    while mod_rest:
        emit_mod_pair(*mod_rest.pop(0))
    mod_bc_final = mst["bc"]
    bc_ga = [t_clamp[0]]
    bc_gb = [t_clamp[1]]
    modB = pieces_done[3] + pieces_done[4]
    v_pe_last, v_last = return_vals
    hT_done = [hT_last[NT - 1]]

    kt_last = None
    kt_pe_last = None
    for half in range(NDH):
        for jd in range(DHALF // 128):
            d_chunk = half * (DHALF // 128) + jd
            for sb in range(S // 512):
                bank = sb % 2
                last = None
                for j in range(MC):
                    last = p.op("pe", [t_wkF] + hT_done +
                                (p.bank_dep(bank) if j == 0 else []),
                                lambda t, j=j, jd=jd, sb=sb, bank=bank, half=half:
                                t.matmul(
                                    pb[bank][:, :],
                                    WkF[:, j * D + half * DHALF + jd * 128:
                                        j * D + half * DHALF + (jd + 1) * 128],
                                    hT[:, j * S + sb * 512:j * S + (sb + 1) * 512],
                                    start=(j == 0), stop=(j == MC - 1)))
                kt_pe_last = last
                ev = p.op("dv", [last] +
                          ([wm_pe[NCC * MC - 1]] if half == 1 else []),
                          lambda v, d_chunk=d_chunk, sb=sb, bank=bank:
                          v.tensor_copy(
                              KT[:, d_chunk * S + sb * 512:d_chunk * S + (sb + 1) * 512],
                              pb[bank][:, :]))
                p.bank_done(ev, bank)
                kt_last = ev

    if dbg:
        p.dma("sp", "q_pst", [v_last], lambda s: s.dma_start(dbg_V.ap(), V[:]))

    xi2_tok = [None] * NT
    xi2_free = {}
    spill_tok = {}
    pt_tr_last = [None] * 4
    exp_last_par = [None] * 4
    AC = {0: dict(cm=8, acc=12, rm=4, nm=5, rs=6, ri=7),
          1: dict(cm=16, acc=20, rm=24, nm=25, rs=26, ri=27),
          2: dict(cm=28, acc=32, rm=36, nm=37, rs=38, ri=39),
          3: dict(cm=40, acc=44, rm=48, nm=49, rs=50, ri=51)}
    pend = None
    qt_pe_final = [None]
    scores_final = [None]
    x1_tok = [None] * NT

    def emit_ptav(ctx):
        qi, exp_toks, nb = ctx
        par = qi % 4
        C = AC[par]
        slot = qi % 2
        Ptb = Pts[par]
        palt = (2 * qi) % 6
        ngr = (qi + 1 + 3) // 4
        evs = []
        tr_last = None
        for g4 in range(ngr):
            bank = 6 if g4 % 2 == 0 else palt
            kts = list(range(g4 * 4, min(g4 * 4 + 4, qi + 1)))
            for k_i, kt in enumerate(kts):
                tr_last = p.op("pe",
                               [exp_toks[min(kt * 128 // 512, nb - 1)], t_idb] +
                               (p.bank_dep(bank) if k_i == 0 else []),
                               lambda t, kt=kt, k_i=k_i, bank=bank: t.transpose(
                                   pbb[bank][:, k_i * 128:(k_i + 1) * 128],
                                   Ptb[:, kt * 128:(kt + 1) * 128], identb[:]))
            nk = len(kts)
            ev = p.op("dv", [tr_last], lambda v, g4=g4, nk=nk, bank=bank:
                      v.tensor_copy(PT[:, g4 * 512:g4 * 512 + nk * 128],
                                    pbb[bank][:, 0:nk * 128]))
            p.bank_done(ev, bank)
            evs.append(ev)
        pt_tr_last[par] = tr_last
        tosb = []
        for mh, (m0, mw) in enumerate(MH):
            bank_av = 7 if mh == 0 else 6
            av_last = None
            for kt in range(qi + 1):
                dep = [evs[kt // 4]] if kt % 4 == 0 else []
                if kt == 0:
                    dep = dep + [v_last] + p.bank_dep(bank_av)
                av_last = p.op("pe", dep,
                               lambda t, kt=kt, qi=qi, m0=m0, mw=mw,
                               bank_av=bank_av: t.matmul(
                                   pb[bank_av][:, 0:mw],
                                   PT[:, kt * 128:(kt + 1) * 128],
                                   V[:, kt * D + m0:kt * D + m0 + mw],
                                   start=(kt == 0), stop=(kt == qi)))
            ts_ = p.op("ac", [av_last], lambda s, m0=m0, mw=mw, C=C,
                       bank_av=bank_av: s.mul(
                           o_sb[:, m0:m0 + mw], pb[bank_av][:, 0:mw],
                           scr[:, C["ri"]:C["ri"] + 1]))
            p.bank_done(ts_, bank_av)
            tosb.append(ts_)
        tga = p.op("dv", tosb + bc_ga, lambda v: v.tensor_tensor(
            o_sb[:], o_sb[:], mslot(2), ALU.mult))
        if dbg:
            dbg_tok[0] = p.dma("sp", "q_pst", [tga, dbg_tok[0]],
                               lambda s, qi=qi: s.dma_start(
                                   dbg_o.ap()[qi * 128:(qi + 1) * 128, :],
                                   o_sb[:]))
        txa = p.op("dv", [tga, xi2_tok[qi], spill_tok.get(qi - 1)] +
                   (dbg_tok if dbg else []),
                   lambda v, slot=slot:
                   v.tensor_tensor(xin[:, slot * D:(slot + 1) * D], o_sb[:],
                                   xin[:, slot * D:(slot + 1) * D], ALU.add))
        xi2_free[slot] = txa
        spill_tok[qi] = p.dma("sp", "q_x", [txa], lambda s, qi=qi, slot=slot:
                              s.dma_start(
                                  x1_d.ap()[qi * 128:(qi + 1) * 128, :],
                                  xin[:, slot * D:(slot + 1) * D]))

    for qi in range(NT):
        slot = qi % 2
        par = qi % 4
        C = AC[par]
        dep = [xi2_free.get(slot) if qi >= 2 else xin_free.get(slot),
               spill_tok.get(qi - 2),
               xi2_tok[qi - 1] if qi else xin_tok[NT - 1]]
        td = p.dma("sp", "q_x", dep, lambda s, qi=qi, slot=slot: s.dma_start(
            xin[:, slot * D:(slot + 1) * D],
            x_in.ap()[qi * 128:(qi + 1) * 128, :]))
        xi2_tok[qi] = td
        keys = (qi + 1) * 128
        kblocks = splits(keys, 512)
        nb = len(kblocks)
        pool = [(2 * qi + kb) % 6 for kb in range(nb)]
        if qi % 4 == 0:
            b4 = qi // 4
            qt_ev = []
            for jd in range(MC):
                bank = pool[jd % min(2, nb)]
                last = None
                for j in range(MC):
                    last = p.op("pe", wq_toks + hT_done +
                                (p.bank_dep(bank) if j == 0 else []),
                                lambda t, j=j, jd=jd, b4=b4, bank=bank: t.matmul(
                                    pb[bank][:, :],
                                    Wqb[:, j * D + jd * 128:j * D + (jd + 1) * 128],
                                    hT[:, j * S + b4 * 512:j * S + (b4 + 1) * 512],
                                    start=(j == 0), stop=(j == MC - 1)))
                ev = p.op("dv", [last] +
                          ([mod_bc_final] if b4 == 0 and jd == 0 else []),
                          lambda v, jd=jd, bank=bank: v.tensor_copy(
                              qt4[:, jd * 512:(jd + 1) * 512], pb[bank][:, :]))
                p.bank_done(ev, bank)
                qt_ev.append(ev)
            qt_pe_final[0] = last
        blk_pe = [None] * nb
        for kb in list(range(2, nb)) + list(range(min(2, nb))):
            k0, kw = kblocks[kb]
            bank = pool[kb]
            last = None
            for jd in range(MC):
                last = p.op("pe", qt_ev + [kt_last] +
                            (p.bank_dep(bank) if jd == 0 else []),
                            lambda t, jd=jd, k0=k0, kw=kw, bank=bank, qi=qi:
                            t.matmul(
                                pb[bank][:, 0:kw],
                                qt4[:, jd * 512 + (qi % 4) * 128:
                                    jd * 512 + (qi % 4) * 128 + 128],
                                KT[:, jd * S + k0:jd * S + k0 + kw],
                                start=(jd == 0), stop=(jd == MC - 1)))
            blk_pe[kb] = last
        if qi == NT - 1:
            scores_final[0] = blk_pe[-1]
        lw = kblocks[-1][1]
        lbank = pool[nb - 1]
        t_mask = p.op("dv", [blk_pe[-1], t_dm], lambda v, lbank=lbank, lw=lw:
                      v.scalar_tensor_tensor(pb[lbank][:, lw - 128:lw],
                                             strictU[:], float(NEG),
                                             pb[lbank][:, lw - 128:lw],
                                             ALU.mult, ALU.add))
        cm_toks = []
        for kb, (k0, kw) in enumerate(kblocks):
            dep = [t_mask] if kb == nb - 1 else [blk_pe[kb]]
            cm_toks.append(p.op("dv", dep, lambda v, kb=kb, kw=kw, C=C,
                                bank=pool[kb]: v.reduce_max(
                                    scr[:, C["cm"] + kb:C["cm"] + kb + 1],
                                    pb[bank][:, 0:kw], axis=AX.X)))
        t_rm = p.op("dv", cm_toks, lambda v, nb=nb, C=C: v.reduce_max(
            scr[:, C["rm"]:C["rm"] + 1], scr[:, C["cm"]:C["cm"] + nb], axis=AX.X))
        t_nm = p.op("dv", [t_rm, exp_last_par[par]], lambda v, C=C: v.tensor_scalar(
            scr[:, C["nm"]:C["nm"] + 1], scr[:, C["rm"]:C["rm"] + 1],
            -ISQ, None, ALU.mult))
        exp_toks = []
        for kb, (k0, kw) in enumerate(kblocks):
            ex = p.op("ac", [t_nm, pt_tr_last[par]],
                      lambda s, kb=kb, k0=k0, kw=kw, C=C, par=par,
                      bank=pool[kb]: s.activation(
                          Pts[par][:, k0:k0 + kw], pb[bank][:, 0:kw], AF.Exp,
                          bias=scr[:, C["nm"]:C["nm"] + 1], scale=ISQ,
                          accum_out=scr[:, C["acc"] + kb:C["acc"] + kb + 1]))
            p.bank_done(ex, pool[kb])
            exp_toks.append(ex)
        exp_last_par[par] = exp_toks[-1]
        if dbg and qi == 1:
            p.dma("sp", "q_pst", exp_toks, lambda s, keys=keys: s.dma_start(
                dbg_P.ap()[:, 0:keys], Pts[1][:, 0:keys]))
            p.dma("sp", "q_pst", exp_toks + qt_ev + [t_nm],
                  lambda s, qi=qi: s.dma_start(
                      dbg_q.ap(),
                      qt4[:].rearrange("q (jd four d) -> q jd four d",
                                       four=4, d=128)[:, :, qi % 4]))
            p.dma("sp", "q_pst", exp_toks + [t_nm],
                  lambda s: s.dma_start(dbg_nm.ap(), scr[:, 24:26]))
        t_rs = p.op("dv", exp_toks, lambda v, nb=nb, C=C: v.tensor_reduce(
            scr[:, C["rs"]:C["rs"] + 1], scr[:, C["acc"]:C["acc"] + nb],
            axis=AX.X, op=ALU.add))
        p.op("dv", [t_rs], lambda v, C=C: v.reciprocal(
            scr[:, C["ri"]:C["ri"] + 1], scr[:, C["rs"]:C["rs"] + 1]))
        if pend is not None:
            emit_ptav(pend)
        pend = (qi, exp_toks, nb)
        if qi == 13:
            for b4 in range(2):
                td_r = p.dma("sp", "q_x",
                             [qt_pe_final[0]] +
                             [spill_tok[b4 * 4 + k] for k in range(4)],
                             lambda s, b4=b4: s.dma_start(
                                 x1[:, b4 * 4 * D:(b4 + 1) * 4 * D],
                                 x1_d.ap()[b4 * 512:(b4 + 1) * 512, :]
                                 .rearrange("(a q) d -> q a d", q=128)))
                for i_ in range(4):
                    x1_tok[b4 * 4 + i_] = td_r
    emit_ptav(pend)
    attn_done = spill_tok[NT - 1]

    out_add = {i: None for i in range(NT)}
    gctr = [0]
    wg_tok = [None, None]
    wg_free = [None, None]

    def load_group(e, g, extra=()):
        buf = gctr[0] % 2
        gctr[0] += 1
        deps = [wg_free[buf]] + list(extra)
        d1 = p.dma("gp", f"q_wg{buf}a", deps, lambda gp, e=e, g=g, buf=buf:
                   gp.dma_start(wgbuf[buf][0][:],
                                w1_in.ap()[e, :, g * 512:(g + 1) * 512]
                                .rearrange("(j q) f -> q j f", q=128)))
        d3 = p.dma("gp", f"q_wg{buf}b", deps, lambda gp, e=e, g=g, buf=buf:
                   gp.dma_start(wgbuf[buf][1][:],
                                w3_in.ap()[e, :, g * 512:(g + 1) * 512]
                                .rearrange("(j q) f -> q j f", q=128)))
        wg_tok[buf] = (d1, d3)
        return buf

    load_group(0, 0, extra=[qt_pe_final[0]])
    if FG > 1:
        load_group(0, 1, extra=[v_pe_last, pt_tr_last[2], pt_tr_last[3]])

    for b4 in range(2, NT // 4):
        td_r = p.dma("sp", "q_x",
                     [scores_final[0]] +
                     [spill_tok[b4 * 4 + k] for k in range(4)],
                     lambda s, b4=b4: s.dma_start(
                         x1[:, b4 * 4 * D:(b4 + 1) * 4 * D],
                         x1_d.ap()[b4 * 512:(b4 + 1) * 512, :]
                         .rearrange("(a q) d -> q a d", q=128)))
        for i_ in range(4):
            x1_tok[b4 * 4 + i_] = td_r

    lg_tok = []
    rt_toks = []
    h2_done = None
    prev_ts1 = None
    router_last = {}
    tcast_tok = [None] * NT
    pend_lg = None

    def emit_routing(j, ev_lg):
        sl = lg_a[:, j * E:(j + 1) * E]
        t1 = p.op("dv", [ev_lg], lambda v, j=j, sl=sl: v.reduce_max(
            scr[:, 16 + j:17 + j], sl, axis=AX.X))
        p1 = p.op("dv", [t1], lambda v, j=j, sl=sl: v.tensor_scalar(
            probs_a[:, j * E:(j + 1) * E], sl, scr[:, 16 + j:17 + j],
            None, ALU.subtract))
        a_ = p.op("dv", [p1], lambda v, j=j, sl=sl: v.tensor_scalar(
            mk_a[:, j * E:(j + 1) * E], sl, scr[:, 16 + j:17 + j],
            None, ALU.is_ge))
        e1 = p.op("ac", [p1], lambda s, j=j: s.activation(
            probs_a[:, j * E:(j + 1) * E], probs_a[:, j * E:(j + 1) * E],
            AF.Exp, accum_out=scr[:, 32 + j:33 + j]))
        b_ = p.op("dv", [a_], lambda v, j=j, sl=sl: v.scalar_tensor_tensor(
            pos_a[:, j * E:(j + 1) * E], mk_a[:, j * E:(j + 1) * E],
            float(NEG), sl, ALU.mult, ALU.add))
        c_ = p.op("dv", [b_], lambda v, j=j: v.reduce_max(
            scr[:, 16 + j:17 + j], pos_a[:, j * E:(j + 1) * E], axis=AX.X))
        d_ = p.op("dv", [c_], lambda v, j=j: v.tensor_scalar(
            pos_a[:, j * E:(j + 1) * E], pos_a[:, j * E:(j + 1) * E],
            scr[:, 16 + j:17 + j], None, ALU.is_ge))
        m_ = p.op("dv", [d_], lambda v, j=j: v.tensor_tensor(
            mk_a[:, j * E:(j + 1) * E], mk_a[:, j * E:(j + 1) * E],
            pos_a[:, j * E:(j + 1) * E], ALU.add))
        r_ = p.op("dv", [e1], lambda v, j=j: v.reciprocal(
            scr[:, 32 + j:33 + j], scr[:, 32 + j:33 + j]))
        n_ = p.op("dv", [r_], lambda v, j=j: v.tensor_scalar(
            probs_a[:, j * E:(j + 1) * E], probs_a[:, j * E:(j + 1) * E],
            scr[:, 32 + j:33 + j], None, ALU.mult))
        pbc = p.op("dv", [n_], lambda v, j=j: v.tensor_copy(
            probs_b[:, j * E:(j + 1) * E], probs_a[:, j * E:(j + 1) * E]))
        tt = p.op("pe", [m_, t_idf] + p.bank_dep(6),
                  lambda t, j=j: t.transpose(
                      pb[6][0:E, 0:128], mk_a[:, j * E:(j + 1) * E], identf[:]))
        ev = p.op("dv", [tt], lambda v, j=j: v.tensor_copy(
            mkT_f[:, j * 128:(j + 1) * 128], pb[6][0:E, 0:128]))
        p.bank_done(ev, 6)
        return pbc, ev
    for i in range(NT):
        par = i % 2
        h2f = h2f_scr[par]
        h2T = h2T_scr[par]
        xt = x1[:, i * D:(i + 1) * D]
        s3_, prev_ts1 = ln_stats(xt, x1_tok[i], prev_ts1, par)
        tl = p.op("ac", [s3_, kt_pe_last, router_last.get(i - 2),
                         tcast_tok[i - 2] if i >= 2 else None],
                  lambda s, xt=xt, h2f=h2f, par=par:
                  s.activation(h2f[:], xt, AF.Identity,
                               bias=scr[:, 3 + 8 * par:4 + 8 * par],
                               scale=scr[:, 2 + 8 * par:3 + 8 * par]))
        tm = p.op("gp", [tl] + modB, lambda g, h2f=h2f: g.tensor_tensor(
            h2f[:], h2f[:], mslot(3), ALU.mult))
        th = p.op("gp", [tm], lambda g, h2f=h2f: g.tensor_tensor(
            h2f[:], h2f[:], mslot(4), ALU.add))
        tcast = p.op("ac", [th, attn_done], lambda v, i=i, h2f=h2f: v.copy(
            h2b[:, i * D:(i + 1) * D], h2f[:]))
        tcast_tok[i] = tcast
        h2_done = tcast
        trs = []
        for jg in range(0, MC, 4):
            ng = min(4, MC - jg)
            bank = 2 * par + (jg // 4) % 2
            tt = None
            for j_i in range(ng):
                tt = p.op("pe", [th, t_idf] +
                          (p.bank_dep(bank) if j_i == 0 else []),
                          lambda t, j=jg + j_i, j_i=j_i, bank=bank, h2f=h2f:
                          t.transpose(
                              pb[bank][:, j_i * 128:(j_i + 1) * 128],
                              h2f[:, j * 128:(j + 1) * 128], identf[:]))
            ev = p.op("ac", [tt], lambda s, jg=jg, ng=ng, bank=bank, h2T=h2T:
                      s.copy(h2T[:, jg * 128:(jg + ng) * 128],
                             pb[bank][:, 0:ng * 128]))
            p.bank_done(ev, bank)
            trs.append(ev)
        last = None
        for j in range(MC):
            last = p.op("pe", [trs[j // 4]] + (t_gwT if i == 0 else []) +
                        (p.bank_dep(4 + par) if j == 0 else []),
                        lambda t, j=j, par=par, h2T=h2T: t.matmul(
                            pb[4 + par][:, 0:E],
                            h2T[:, j * 128:(j + 1) * 128],
                            gwT[:, j * E:(j + 1) * E],
                            start=(j == 0), stop=(j == MC - 1)))
        router_last[i] = last
        if pend_lg is not None:
            j, jpar = pend_lg
            ev = p.op("dv", [router_last[j]], lambda v, j=j, jpar=jpar:
                      v.tensor_copy(lg_a[:, j * E:(j + 1) * E],
                                    pb[4 + jpar][:, 0:E]))
            p.bank_done(ev, 4 + jpar)
            lg_tok.append(ev)
            rt_toks.append(emit_routing(j, ev))
        pend_lg = (i, par)
    j, jpar = pend_lg
    ev = p.op("dv", [router_last[j]], lambda v, j=j, jpar=jpar: v.tensor_copy(
        lg_a[:, j * E:(j + 1) * E], pb[4 + jpar][:, 0:E]))
    p.bank_done(ev, 4 + jpar)
    lg_tok.append(ev)
    rt_toks.append(emit_routing(j, ev))
    norm_last = rt_toks[-1][0]

    t_scan = p.op("dv", [rt_toks[-1][1]], lambda v: v.tensor_tensor_scan(
        mkT_f[:], mkT_f[:], mkT_f[:], 0.0, ALU.add, ALU.bypass))
    C_last = None
    for i in range(NT):
        tt = p.op("pe", [t_scan, t_idf] + p.bank_dep(3),
                  lambda t, i=i: t.transpose(
                      pb[3][:, 0:E], mkT_f[0:E, i * 128:(i + 1) * 128],
                      identf[0:E, 0:E]))
        C_last = p.op("dv", [tt], lambda v, i=i: v.tensor_copy(
            C_a[:, i * E:(i + 1) * E], pb[3][:, 0:E]))
        p.bank_done(C_last, 3)
    C_last = p.op("dv", [C_last], lambda v: v.tensor_tensor(
        C_a[:], C_a[:], mk_a[:], ALU.subtract))
    tv1 = p.op("dv", [C_last], lambda v: v.tensor_scalar(
        pos_a[:], C_a[:], float(CAP), None, ALU.is_lt))
    tv2 = p.op("dv", [tv1], lambda v: v.tensor_tensor(
        pos_a[:], pos_a[:], mk_a[:], ALU.mult))
    tv3 = p.op("dv", [tv2], lambda v: v.tensor_scalar(
        C_a[:], C_a[:], 1.0, None, ALU.add))
    tv4 = p.op("dv", [tv3], lambda v: v.tensor_tensor(
        pos_a[:], pos_a[:], C_a[:], ALU.mult))
    tpos = p.op("dv", [tv4], lambda v: v.tensor_scalar(
        pos_a[:], pos_a[:], 1.0, None, ALU.subtract))
    posT_last = None
    for i in range(NT):
        tt = p.op("pe", [tpos, t_idf] + p.bank_dep(3),
                  lambda t, i=i: t.transpose(
                      pb[3][0:E, 0:128], pos_a[:, i * E:(i + 1) * E], identf[:]))
        posT_last = p.op("dv", [tt], lambda v, i=i: v.tensor_copy(
            posT[:, i * 128:(i + 1) * 128], pb[3][0:E, 0:128]))
        p.bank_done(posT_last, 3)

    w2_free = [None, None]

    def load_w2(e, g, half, lastbuf, deps):
        w1g, w3g = wgbuf[lastbuf]
        dst = w1g if half == 0 else w3g
        tok = p.dma("gp", f"q_w2{half}", [w2_free[half]] + deps,
                    lambda gp, e=e, g=g, dst=dst: gp.dma_start(
                        dst[:].rearrange("q (fj m) -> q fj m", fj=4),
                        w2_in.ap()[e, g * 512:(g + 1) * 512, :]
                        .rearrange("(fj q) m -> q fj m", q=128)))
        return tok, dst

    de_free = None
    combine_pe_last = None
    sel_free = [None]
    w2s_war = [None]

    first_expert_deps = [attn_done, h2_done, tpos, norm_last]

    for e in range(E):
        disp_ev_last = None
        ps_row_ev = []
        de_pe_last = None
        for chalf in range(2):
            de_toks = []
            for i in range(NT):
                dep = [tpos, t_iotac]
                if i == 0:
                    dep += [de_pe_last if chalf == 1 else de_free]
                    if chalf == 0:
                        dep += [w2s_war[0]]
                de_toks.append(p.op("dv", dep,
                                    lambda v, i=i, e=e, chalf=chalf:
                                    v.tensor_scalar(
                                        De[:, i * CH:(i + 1) * CH],
                                        iotac[:, 0:CH],
                                        pos_a[:, i * E + e:i * E + e + 1],
                                        float(-chalf * CH),
                                        ALU.subtract, ALU.is_equal)))
            for j in range(MC):
                bank = j % 2
                last = None
                for i in range(NT):
                    dep = [de_toks[i]]
                    if i == 0:
                        dep += p.bank_dep(bank)
                        if j == 0 and chalf == 0:
                            dep += first_expert_deps if e == 0 else [combine_pe_last]
                    last = p.op("pe", dep, lambda t, i=i, j=j, bank=bank: t.matmul(
                        pb[bank][:, 0:CH],
                        h2b[:, i * D + j * 128:i * D + (j + 1) * 128],
                        De[:, i * CH:(i + 1) * CH],
                        start=(i == 0), stop=(i == NT - 1)))
                ev = p.op("dv", [last], lambda v, j=j, chalf=chalf, bank=bank:
                          v.tensor_copy(
                              disp[:, j * CAP + chalf * CH:j * CAP + (chalf + 1) * CH],
                              pb[bank][:, 0:CH]))
                p.bank_done(ev, bank)
                disp_ev_last = ev
            last = None
            for i in range(NT):
                dep = [de_toks[i]]
                if i == 0:
                    dep += p.bank_dep(2)
                last = p.op("pe", dep, lambda t, i=i, e=e: t.matmul(
                    pb[2][0:1, 0:CH], probs_b[:, i * E + e:i * E + e + 1],
                    De[:, i * CH:(i + 1) * CH],
                    start=(i == 0), stop=(i == NT - 1)))
            de_pe_last = last
            ev = p.op("dv", [last], lambda v, chalf=chalf: v.tensor_copy(
                prob_row[0:1, chalf * CH:(chalf + 1) * CH], pb[2][0:1, 0:CH]))
            p.bank_done(ev, 2)
            ps_row_ev.append(ev)
        de_free = de_pe_last
        t_sel = p.op("dv", [t_iotaP, posT_last, sel_free[0]],
                     lambda v, e=e: v.tensor_scalar(
                         sel[:], iotaP[:], float(e), None, ALU.is_equal))

        def emit_cm_half(sh):
            shs = S // SH
            out = {}
            for sc, (sc0, scw) in enumerate(splits(shs, 512)):
                scg = sh * shs + sc0
                bank = 6 + sc
                mm = p.op("pe", [t_sel] + p.bank_dep(bank),
                          lambda t, scg=scg, scw=scw, bank=bank: t.matmul(
                              pb[bank][:, 0:scw], sel[:],
                              posT[0:E, scg:scg + scw],
                              start=True, stop=True))
                sel_free[0] = mm
                cm_last = None
                for cchunk in range(CC):
                    wdt = min(128, CAP - cchunk * 128)
                    cm_last = p.op("dv", [mm, t_ccol],
                                   lambda v, cchunk=cchunk, sc0=sc0, scw=scw,
                                   wdt=wdt, bank=bank: v.tensor_scalar(
                                       CM[0:wdt, cchunk * shs + sc0:
                                          cchunk * shs + sc0 + scw],
                                       pb[bank][0:wdt, 0:scw],
                                       ccol[0:wdt, cchunk:cchunk + 1],
                                       None, ALU.is_equal))
                p.bank_done(cm_last, bank)
                out[(sh, sc)] = cm_last
            return out

        cm_blocks = {}
        pcol_toks = []
        for cchunk in range(CC):
            tt = p.op("pe", ps_row_ev + [t_onesb] + p.bank_dep(3),
                      lambda t, cchunk=cchunk: t.matmul(
                          pb[3][:, 0:1],
                          prob_row[0:1, cchunk * 128:(cchunk + 1) * 128],
                          ones_b[0:1, 0:1], start=True, stop=True))
            ev = p.op("dv", [tt], lambda v, cchunk=cchunk: v.tensor_copy(
                scr[:, 20 + cchunk:21 + cchunk], pb[3][:, 0:1]))
            p.bank_done(ev, 3)
            pcol_toks.append(ev)

        g_last = None
        h_pe_last = None
        for g in range(FG):
            cur = (e * FG + g) % 2
            tw1, tw3 = wg_tok[cur]
            w1g, w3g = wgbuf[cur]
            for fi in range(4):
                fc = g * 4 + fi
                par = fc % 2
                b1 = 0 if par == 0 else 4
                b3 = 2 if par == 0 else 6
                h1_last = None
                h3_last = None
                for j in range(MC):
                    for cs, (c0, cw) in enumerate(CS):
                        dep = [tw1]
                        if j == 0:
                            dep += p.bank_dep(b1 + cs)
                            if cs == 0:
                                dep += [disp_ev_last]
                        h1_last = p.op("pe", dep,
                                       lambda t, j=j, fi=fi, c0=c0, cw=cw, b1=b1,
                                       cs=cs, w1g=w1g: t.matmul(
                                           pb[b1 + cs][:, 0:cw],
                                           w1g[:, j * 512 + fi * 128:j * 512 + (fi + 1) * 128],
                                           disp[:, j * CAP + c0:j * CAP + c0 + cw],
                                           start=(j == 0), stop=(j == MC - 1)))
                for j in range(MC):
                    for cs, (c0, cw) in enumerate(CS):
                        dep = [tw3]
                        if j == 0:
                            dep += p.bank_dep(b3 + cs)
                        h3_last = p.op("pe", dep,
                                       lambda t, j=j, fi=fi, c0=c0, cw=cw, b3=b3,
                                       cs=cs, w3g=w3g: t.matmul(
                                           pb[b3 + cs][:, 0:cw],
                                           w3g[:, j * 512 + fi * 128:j * 512 + (fi + 1) * 128],
                                           disp[:, j * CAP + c0:j * CAP + c0 + cw],
                                           start=(j == 0), stop=(j == MC - 1)))
                h_pe_last = h3_last
                sl = []
                for cs, (c0, cw) in enumerate(CS):
                    sl.append(p.op("ac", [h1_last, g_last], lambda s, c0=c0,
                                   cw=cw, b1=b1, cs=cs: s.activation(
                                       s1[:, c0:c0 + cw], pb[b1 + cs][:, 0:cw],
                                       AF.Silu)))
                for cs in range(len(CS)):
                    p.bank_done(sl[cs], b1 + cs)
                gt = None
                for cs, (c0, cw) in enumerate(CS):
                    gt = p.op("dv", [sl[cs], h3_last],
                              lambda v, fc=fc, c0=c0, cw=cw, b3=b3, cs=cs:
                              v.tensor_tensor(
                                  gT[:, fc * CAP + c0:fc * CAP + c0 + cw],
                                  s1[:, c0:c0 + cw], pb[b3 + cs][:, 0:cw],
                                  ALU.mult))
                    p.bank_done(gt, b3 + cs)
                g_last = gt
            wg_free[cur] = h3_last
            if g + 2 < FG:
                load_group(e, g + 2)

        lastbuf = (e * FG + FG - 1) % 2
        otherbuf = 1 - lastbuf
        cgroups = []
        ci = 0
        while ci < CC:
            cgroups.append(list(range(ci, min(ci + 3, CC))))
            ci += 3
        ring = [w2s, wgbuf[lastbuf][0], wgbuf[lastbuf][1],
                wgbuf[otherbuf][0], wgbuf[otherbuf][1]]
        ring_free = [w2s_war[0], wg_free[lastbuf], wg_free[lastbuf],
                     wg_free[otherbuf], wg_free[otherbuf]]
        ring_free[0] = de_free if ring_free[0] is None else ring_free[0]
        oe_ev_last = None
        w2_pe_last = None
        w2idx = 0
        for gi, cg in enumerate(cgroups):
            mm_last = {}
            for g in range(FG):
                slot = w2idx % 5
                w2idx += 1
                tok = p.dma("gp", f"q_w2{slot}", [ring_free[slot]] +
                            ([de_free] if slot == 0 else []),
                            lambda gp, e=e, g=g, dst=ring[slot]: gp.dma_start(
                                dst[:].rearrange("q (fj m) -> q fj m", fj=4),
                                w2_in.ap()[e, g * 512:(g + 1) * 512, :]
                                .rearrange("(fj q) m -> q fj m", q=128)))
                w2t = ring[slot]
                for fi in range(4):
                    fc = g * 4 + fi
                    for ti, ct_ in enumerate(cg):
                        wdt = min(128, CAP - ct_ * 128)
                        for mh, (m0, mw) in enumerate(MH):
                            bank = 2 * ti + mh if len(MH) > 1 else ti
                            dep = [tok]
                            if fc == 0:
                                dep += [g_last]
                                dep += p.bank_dep(bank)
                            mm = p.op("pe", dep,
                                      lambda t, fc=fc, fi=fi, ti=ti, ct_=ct_,
                                      wdt=wdt, m0=m0, mw=mw, bank=bank, w2t=w2t:
                                      t.matmul(
                                          pb[bank][:, 0:mw],
                                          gT[:, fc * CAP + ct_ * 128:fc * CAP + ct_ * 128 + wdt],
                                          w2t[:, fi * D + m0:fi * D + m0 + mw],
                                          start=(fc == 0), stop=(fc == FC - 1)))
                            mm_last[(ti, mh)] = mm
                            w2_pe_last = mm
                ring_free[slot] = w2_pe_last
                if slot == 0:
                    w2s_war[0] = w2_pe_last
                if gi == len(cgroups) - 1 and g == 1:
                    cm_blocks.update(emit_cm_half(0))
            for ti, ct_ in enumerate(cg):
                wdt = min(128, CAP - ct_ * 128)
                for mh, (m0, mw) in enumerate(MH):
                    bank = 2 * ti + mh if len(MH) > 1 else ti
                    e1 = p.op("dv", [mm_last[(ti, mh)], pcol_toks[ct_]] +
                              ([h_pe_last] if gi == 0 and ti == 0 and mh == 0 else []),
                              lambda v, ct_=ct_, wdt=wdt, m0=m0, mw=mw, bank=bank:
                              v.tensor_scalar(
                                  out_e[0:wdt, ct_ * D + m0:ct_ * D + m0 + mw],
                                  pb[bank][0:wdt, 0:mw],
                                  scr[0:wdt, 20 + ct_:21 + ct_], None, ALU.mult))
                    e2 = p.op("dv", [e1] + bc_gb,
                              lambda v, ct_=ct_, wdt=wdt, m0=m0, mw=mw:
                              v.tensor_tensor(
                                  out_e[0:wdt, ct_ * D + m0:ct_ * D + m0 + mw],
                                  out_e[0:wdt, ct_ * D + m0:ct_ * D + m0 + mw],
                                  Mb[0:wdt, 5 * D + m0:5 * D + m0 + mw],
                                  ALU.mult))
                    p.bank_done(e2, bank)
                    oe_ev_last = e2
        if e + 1 < E:
            slot_of = {lastbuf: (1, 2), otherbuf: (3, 4)}
            b0 = ((e + 1) * FG) % 2
            wg_free[b0] = ring_free[slot_of[b0][0]]
            load_group(e + 1, 0, extra=[ring_free[slot_of[b0][1]]])
            if FG > 1:
                b1 = ((e + 1) * FG + 1) % 2
                wg_free[b1] = ring_free[slot_of[b1][0]]
                load_group(e + 1, 1, extra=[ring_free[slot_of[b1][1]]])

        for sh in range(SH):
            shs = S // SH
            if sh > 0:
                cm_blocks.update(emit_cm_half(sh))
            for i in range(sh * (NT // SH), (sh + 1) * (NT // SH)):
                i_in = i - sh * (NT // SH)
                bank = 4 + 2 * (i % 2) if len(MH) > 1 else 4 + (i % 2)
                last = None
                for cchunk in range(CC):
                    wdt = min(128, CAP - cchunk * 128)
                    for mh, (m0, mw) in enumerate(MH):
                        dep = []
                        if cchunk == 0 and mh == 0:
                            dep = [cm_blocks[(sh, i_in * 128 // 512)]]
                            for mh2 in range(len(MH)):
                                dep += p.bank_dep(bank + mh2)
                        last = p.op("pe", dep,
                                    lambda t, i_in=i_in, cchunk=cchunk, m0=m0,
                                    mw=mw, mh=mh, wdt=wdt, bank=bank: t.matmul(
                                        pb[bank + mh][:, 0:mw],
                                        CM[0:wdt, cchunk * shs + i_in * 128:
                                           cchunk * shs + (i_in + 1) * 128],
                                        out_e[0:wdt, cchunk * D + m0:
                                              cchunk * D + m0 + mw],
                                        start=(cchunk == 0), stop=(cchunk == CC - 1)))
                combine_pe_last = last
                for mh, (m0, mw) in enumerate(MH):
                    ta_ = p.op("dv", [last, out_add[i]],
                               lambda v, i=i, m0=m0, mw=mw, mh=mh, bank=bank:
                               v.tensor_tensor(
                                   x1[:, i * D + m0:i * D + m0 + mw],
                                   x1[:, i * D + m0:i * D + m0 + mw],
                                   pb[bank + mh][:, 0:mw], ALU.add))
                    p.bank_done(ta_, bank + mh)
                    out_add[i] = ta_

    out_toks = []
    for b2 in range(NT // 2):
        deps = [out_add[b2 * 2 + i_] for i_ in range(2)]
        out_toks.append(p.dma("sp", "q_x" if b2 % 2 == 0 else "q_misc", deps,
                              lambda s, b2=b2: s.dma_start(
                                  out_d.ap()[b2 * 256:(b2 + 1) * 256, :]
                                  .rearrange("(a q) d -> q a d", q=128),
                                  x1[:, b2 * 2 * D:(b2 + 1) * 2 * D])))
    p._emit("sp", out_toks, lambda s: None, None, 0)

    with nc.Block() as blk:
        p.realize(blk)
    nc.finalize()
    return nc


def kernel(L0_attn_x, L0_attn_Wk, L0_attn_Wq, L0_attn_Wv, L0_ffn_gate_w,
           L0_ffn_w1, L0_ffn_w2, L0_ffn_w3, L0_c, L0_wmod):
    B = L0_attn_x.shape[0]
    nc = build_nc()
    in_maps = []
    for b in range(B):
        in_maps.append({
            "x": np.ascontiguousarray(np.asarray(L0_attn_x)[b]),
            "c": np.ascontiguousarray(np.asarray(L0_c)[b:b + 1]),
            "wmod": np.asarray(L0_wmod),
            "wk": np.asarray(L0_attn_Wk),
            "wq": np.asarray(L0_attn_Wq),
            "wv": np.asarray(L0_attn_Wv),
            "gw": np.asarray(L0_ffn_gate_w),
            "w1": np.asarray(L0_ffn_w1),
            "w2": np.asarray(L0_ffn_w2),
            "w3": np.asarray(L0_ffn_w3),
        })
    res = run_bass_kernel_spmd(nc, in_maps, core_ids=list(range(B)))
    return np.stack([np.asarray(r["out"]) for r in res.results], axis=0)


# version 19
# speedup vs baseline: 1.1734x; 1.0181x over previous
import numpy as np
import concourse.bass as bass
import concourse.mybir as mybir
from concourse.bass_utils import run_bass_kernel_spmd

F32 = mybir.dt.float32
BF16 = mybir.dt.bfloat16
F16 = mybir.dt.float16
AF = mybir.ActivationFunctionType
ALU = mybir.AluOpType
AX = mybir.AxisListType

NEG = -1.0e30


def splits(total, step):
    out = []
    o = 0
    while o < total:
        out.append((o, min(step, total - o)))
        o += step
    return out


class PB:

    ENGS = ("pe", "dv", "ac", "gp", "sp")

    def __init__(self, nc):
        self.nc = nc
        self.steps = {e: [] for e in self.ENGS}
        self.sem = {}
        self.cnt = {}
        for e in self.ENGS:
            self.sem[e] = nc.alloc_semaphore(f"s_{e}")
            self.cnt[e] = 0
        for q in ("q_x", "q_wv0", "q_wv1", "q_wk", "q_wg0a", "q_wg0b",
                  "q_wm0", "q_wm1", "q_wm2", "q_wm3",
                  "q_wg1a", "q_wg1b", "q_w20", "q_w21", "q_w22", "q_w23", "q_w24",
                  "q_misc", "q_gw", "q_pst"):
            self.sem[q] = nc.alloc_semaphore(f"s_{q}")
            self.cnt[q] = 0
        self.waited = {e: {} for e in self.ENGS}
        self.selfser = {"dv", "ac", "gp"}
        self.bguard = {}

    def _emit(self, eng, deps, fn, inc_key, inc_amt):
        deps = [d for d in deps if d is not None]
        if eng in self.selfser and self.cnt[eng] > 0:
            deps = deps + [(eng, self.cnt[eng])]
        if inc_key is not None:
            self.cnt[inc_key] += inc_amt
        val_after = self.cnt[inc_key] if inc_key is not None else None
        w = self.waited[eng]
        needed = {}
        for (k, v) in deps:
            if v > w.get(k, 0):
                needed[k] = max(needed.get(k, 0), v)
        for k, v in needed.items():
            w[k] = v
        self.steps[eng].append((sorted(needed.items()), fn, inc_key, inc_amt))
        return (inc_key, val_after) if inc_key is not None else None

    def op(self, eng, deps, fn):
        return self._emit(eng, deps, fn, eng, 1)

    def dma(self, eng, queue, deps, fn):
        return self._emit(eng, deps, fn, queue, 16)

    def bank_dep(self, *banks):
        return [self.bguard.get(b) for b in banks]

    def bank_done(self, tok, *banks):
        for b in banks:
            self.bguard[b] = tok

    def realize(self, block):
        sems = self.sem
        used_pe = set()
        for e in self.ENGS:
            for (needed, _f, _k, _a) in self.steps[e]:
                for (k, v) in needed:
                    if k == "pe":
                        used_pe.add(v)

        def body(engobj, eng):
            pe_val = 0
            pending = 0
            n = len(self.steps[eng])
            for idx, (needed, fn, inc_key, inc_amt) in enumerate(self.steps[eng]):
                for (k, v) in needed:
                    engobj.wait_ge(sems[k], v)
                ins = fn(engobj)
                if inc_key == "pe":
                    pe_val += inc_amt
                    pending += inc_amt
                    if pe_val in used_pe or idx == n - 1:
                        ins.then_inc(sems["pe"], pending)
                        pending = 0
                elif inc_key is not None:
                    ins.then_inc(sems[inc_key], inc_amt)

        block.tensor(lambda t: body(t, "pe"))
        block.vector(lambda v: body(v, "dv"))
        block.scalar(lambda s: body(s, "ac"))
        block.gpsimd(lambda g: body(g, "gp"))
        block.sync(lambda s: body(s, "sp"))


def build_nc(S=2048, D=1024, F=3584, E=8, CAP=640, dbg=False):
    NT = S // 128
    MC = D // 128
    FC = F // 128
    FG = F // 512
    CC = (CAP + 127) // 128
    CH = CAP // 2
    LNC = splits(D, 512)
    MH = splits(D, 512)
    CS = splits(CAP, 512)
    SH = 2 if NT % 2 == 0 else 1
    ISQ = float(1.0 / np.sqrt(np.float32(D)))
    assert CH <= 512 and F % 512 == 0 and S % 512 == 0 and D % 128 == 0

    nc = bass.Bass("TRN2", target_bir_lowering=False)

    x_in = nc.dram_tensor("x", [S, D], F32, kind="ExternalInput")
    c_in = nc.dram_tensor("c", [1, D], F32, kind="ExternalInput")
    wmod_in = nc.dram_tensor("wmod", [D, 6 * D], F32, kind="ExternalInput")
    wk_in = nc.dram_tensor("wk", [D, D], F32, kind="ExternalInput")
    wq_in = nc.dram_tensor("wq", [D, D], F32, kind="ExternalInput")
    wv_in = nc.dram_tensor("wv", [D, D], F32, kind="ExternalInput")
    gw_in = nc.dram_tensor("gw", [E, D], F32, kind="ExternalInput")
    w1_in = nc.dram_tensor("w1", [E, D, F], F32, kind="ExternalInput")
    w2_in = nc.dram_tensor("w2", [E, F, D], F32, kind="ExternalInput")
    w3_in = nc.dram_tensor("w3", [E, D, F], F32, kind="ExternalInput")
    out_d = nc.dram_tensor("out", [S, D], F32, kind="ExternalOutput")
    x1_d = nc.dram_tensor("x1spill", [S, D], F32, kind="Internal")
    dbg_o = nc.dram_tensor("dbg_o", [S, D], F32, kind="Internal") if dbg else None
    dbg_P = nc.dram_tensor("dbg_P", [128, S], F32, kind="Internal") if dbg else None
    dbg_V = nc.dram_tensor("dbg_V", [128, S * D // 128], mybir.dt.bfloat16, kind="Internal") if dbg else None
    dbg_q = nc.dram_tensor("dbg_q", [128, D], mybir.dt.bfloat16, kind="Internal") if dbg else None
    dbg_nm = nc.dram_tensor("dbg_nm", [128, 2], F32, kind="Internal") if dbg else None
    dbg_tok = [None]

    ARENA = 212832
    base = nc.bump_sbuf(ARENA)[0]
    alloc_at = nc.alloc_sbuf_tensor_at
    off = [base]

    def place(sz):
        o = off[0]
        off[0] += (sz + 31) & ~31
        return o

    r0 = place(max(NT * D * 4, 2 * MC * S * 2, 2 * 2048 * 4))
    x1 = alloc_at("x1", [128, NT * D], F32, offset=r0)
    hT = alloc_at("hT", [128, MC * S], BF16, offset=r0)
    KT = alloc_at("KT", [128, MC * S], BF16, offset=r0 + MC * S * 2)
    NSLOT = 4
    wmod_b = alloc_at("wmod_b", [128, NSLOT * 1024], BF16,
                      offset=r0 + NT * D * 4 - NSLOT * 1024 * 2)

    r1 = place(max(NT * D * 2, FC * CAP * 2,
                   CC * (S // SH) * 2 + S * 2 + S * 4, 4 * D * 4))
    V = alloc_at("V", [128, NT * D], BF16, offset=r1)
    gT = alloc_at("gT", [128, FC * CAP], BF16, offset=r1)
    CM = alloc_at("CM", [128, CC * (S // SH)], BF16, offset=r1)

    DHALF = min(512, D)
    NDH = D // DHALF
    atn2 = MC * D * 2 + 2 * MC * DHALF * 2
    r2 = place(max(atn2, 2 * 2 * MC * 512 * 2))
    Wqb = alloc_at("Wqb", [128, MC * D], BF16, offset=r2)
    Wkv = [alloc_at(f"Wkv{h}", [128, MC * DHALF], BF16,
                    offset=r2 + MC * D * 2 + h * MC * DHALF * 2)
           for h in range(2)]
    Pts_hi = [alloc_at(f"Pt{2 + k}", [128, S], BF16,
                       offset=r2 + MC * D * 2 + k * S * 2) for k in range(2)]
    assert 2 * MC * DHALF * 2 >= 2 * S * 2
    GB = MC * 512 * 2
    wgbuf = []
    for i in range(2):
        wgbuf.append((alloc_at(f"w1g{i}", [128, MC * 512], BF16,
                               offset=r2 + i * 2 * GB),
                      alloc_at(f"w3g{i}", [128, MC * 512], BF16,
                               offset=r2 + i * 2 * GB + GB)))

    atn3 = 2 * D * 4 + D * 4 + S * 4 + 4 * D * 2 + NT * 128 * 2
    r3 = place(max(NT * D * 2, atn3))
    h2b = alloc_at("h2b", [128, NT * D], BF16, offset=r3)
    xin = alloc_at("xin", [128, 2 * D], F32, offset=r3)
    o_sb = alloc_at("o_sb", [128, D], F32, offset=r3 + 2 * D * 4)
    Pts = [alloc_at(f"Pt{k}", [128, S], BF16,
                    offset=r3 + 3 * D * 4 + k * S * 2) for k in range(2)] + Pts_hi
    hs1 = alloc_at("hs1", [128, D], F32, offset=r3 + 3 * D * 4)
    qt4 = alloc_at("qt4", [128, 4 * D], BF16, offset=r3 + 3 * D * 4 + S * 4)
    mod_stage = alloc_at("mod_stage", [1, 512], F32, offset=r3 + 3 * D * 4 + S * 4)
    cl_scr = alloc_at("cl_scr", [128, D], F32,
                      offset=r3 + 3 * D * 4 + S * 4 + 2048)
    PT = alloc_at("PT", [128, NT * 128], BF16,
                  offset=r3 + 3 * D * 4 + S * 4 + 4 * D * 2)

    s1win = max(max(CAP, 512) * 2, CAP * 2)
    r4 = place(NT * CH * 2 + s1win)
    De = alloc_at("De", [128, NT * CH], BF16, offset=r4)
    w2s = alloc_at("w2s", [128, MC * 512], BF16, offset=r4)
    s1 = alloc_at("s1", [128, max(CAP, 512)], BF16, offset=r4 + NT * CH * 2)
    prob_row = alloc_at("prob_row", [1, CAP], BF16, offset=r4 + NT * CH * 2)
    assert FC * CAP * 2 >= CC * (S // SH) * 2 + S * 2
    mkT_f = alloc_at("mkT_f", [E, S], F32, offset=r1 + CC * (S // SH) * 2 + S * 2)
    gw_s = alloc_at("gw_s", [E, D], F32, offset=r4)
    WkF = alloc_at("WkF", [128, MC * D], BF16, offset=r4)
    h2f_scr = [alloc_at(f"h2f_scr{k}", [128, D], F32, offset=r4 + k * 2 * D * 4)
               for k in range(2)]
    h2T_scr = [alloc_at(f"h2T_scr{k}", [128, D], F32,
                        offset=r4 + k * 2 * D * 4 + D * 4) for k in range(2)]

    r5 = place(max(MC * CAP * 2, CC * D * 2))
    disp = alloc_at("disp", [128, MC * CAP], BF16, offset=r5)
    out_e = alloc_at("out_e", [128, CC * D], BF16, offset=r5)

    rM = place(6 * D * 2)
    Mb = alloc_at("Mb", [128, 6 * D], BF16, offset=rM)

    def mslot(i):
        return Mb[:, i * D:(i + 1) * D]

    ones_f = alloc_at("ones_f", [1, 128], F32, offset=place(128 * 4))
    iotaP = alloc_at("iotaP", [8, 128], F32, offset=place(128 * 4))
    sel = alloc_at("sel", [8, 128], F16, offset=place(128 * 2))
    ones_b = alloc_at("ones_b", [1, 128], BF16, offset=place(128 * 2))
    posT = alloc_at("posT", [E, S], F16, offset=place(S * 2))

    rR = place(4 * NT * E * 4)
    probs_a = alloc_at("probs_a", [128, NT * E], F32, offset=rR)
    mk_a = alloc_at("mk_a", [128, NT * E], F32, offset=rR + NT * E * 4)
    C_a = alloc_at("C_a", [128, NT * E], F32, offset=rR + 2 * NT * E * 4)
    pos_a = alloc_at("pos_a", [128, NT * E], F32, offset=rR + 3 * NT * E * 4)
    lg_a = C_a

    identf = alloc_at("identf", [128, 128], F32, offset=place(128 * 4))
    identb = alloc_at("identb", [128, 128], BF16, offset=place(128 * 2))
    iw = max(CH, 128)
    iotac = alloc_at("iotac", [128, iw], F32, offset=place(iw * 4))
    pcol = alloc_at("pcol", [128, 1], F32, offset=place(4))
    ccol = alloc_at("ccol", [128, CC], F32, offset=place(CC * 4))
    strictU = alloc_at("strictU", [128, 128], F32, offset=place(128 * 4))
    gwT = alloc_at("gwT", [128, MC * E], F32, offset=place(MC * E * 4))
    ct = alloc_at("ct", [128, MC], F32, offset=place(MC * 4))
    cts = alloc_at("cts", [128, MC], BF16, offset=place(MC * 2))
    scr = alloc_at("scr", [128, 56], F32, offset=place(56 * 4))
    probs_b = alloc_at("probs_b", [128, NT * E], BF16, offset=place(NT * E * 2))
    bn6 = alloc_at("bn6", [128, 12 * len(LNC)], F32, offset=place(12 * len(LNC) * 4))
    st2 = alloc_at("st2", [128, 4], F32, offset=place(4 * 4))
    assert off[0] - base <= ARENA, f"SBUF over budget: {off[0] - base}"

    pb = [nc.alloc_psum_tensor(f"pb{i}", [128, 512], F32) for i in range(8)]
    pbb = [t.bitcast(BF16) for t in pb]

    p = PB(nc)

    def g_iota(ap, pattern, base, chm):
        return lambda g: g.iota(ap, pattern=pattern, base=base,
                                channel_multiplier=chm,
                                allow_small_or_imprecise_dtypes=True)

    t_iotac = p.op("gp", [], g_iota(iotac[:], [[1, iw]], 0, 0))
    t_pcol = p.op("gp", [], g_iota(pcol[:], [[0, 1]], 0, 1))
    t_ccol = p.op("gp", [], g_iota(ccol[:], [[128, CC]], 0, 1))
    t_ones = p.op("gp", [], lambda g: g.memset(ones_f[:], 1.0))
    t_iotaP = p.op("gp", [], g_iota(iotaP[:], [[0, 128]], 0, 1))
    t_onesb = p.op("gp", [], lambda g: g.memset(ones_b[:], 1.0))
    t_eps = p.op("gp", [], lambda g: g.memset(scr[:, 0:1], 1e-5))
    t_ct = p.dma("gp", "q_misc", [], lambda g: g.dma_start(
        ct[:, :], c_in.ap().rearrange("o (j q) -> o q j", q=128)[0],
        allow_slow_non_contiguous=True))
    t_gw = p.dma("gp", "q_gw", [], lambda g: g.dma_start(gw_s[:, :], gw_in.ap()))

    t_idf = p.op("dv", [t_iotac, t_pcol], lambda v: v.tensor_scalar(
        identf[:], iotac[:, 0:128], pcol[:], None, ALU.is_equal))
    t_idb = p.op("dv", [t_idf], lambda v: v.tensor_copy(identb[:], identf[:]))
    t_su = p.op("dv", [t_iotac, t_pcol], lambda v: v.tensor_scalar(
        strictU[:], iotac[:, 0:128], pcol[:], None, ALU.is_gt))
    t_dm = t_su

    t_gwT = []
    for j in range(MC):
        tt = p.op("pe", [t_gw, t_idf] + p.bank_dep(0),
                  lambda t, j=j: t.transpose(
                      pb[0][:, 0:E], gw_s[:, j * 128:(j + 1) * 128],
                      identf[0:E, 0:E]))
        te = p.op("dv", [tt], lambda v, j=j: v.tensor_copy(
            gwT[:, j * E:(j + 1) * E], pb[0][:, 0:E]))
        p.bank_done(te, 0)
        t_gwT.append(te)

    def wdma_half(dst, sem, src, half, deps):
        return p.dma("gp", sem, deps, lambda g: g.dma_start(
            dst[:], src.ap()[:, half * DHALF:(half + 1) * DHALF]
            .rearrange("(j q) d -> q j d", q=128)))

    wv_tok = [wdma_half(Wkv[h], f"q_wv{h}", wv_in, h, []) for h in range(NDH)]

    PIECE_SLOT = {0: 1, 1: 0, 2: 2, 3: 4, 4: 3, 5: 5}
    t_cts = p.op("ac", [t_ct, t_gw], lambda s: s.activation(cts[:], ct[:], AF.Silu))
    SUB = min(512, D)
    WB = 2 * SUB
    NCC = (6 * D) // WB
    pieces_done = {pc: [] for pc in range(6)}
    t_clamp = []
    wm_pe = {}
    mst = {"bc": None}

    def emit_clamp(pc):
        r_ = mslot(pc)
        sg = cl_scr[:]
        a1 = p.op("dv", pieces_done[pc] + t_clamp,
                  lambda v, r_=r_, sg=sg: v.tensor_scalar(
                      sg, r_, 0.0, 2.0, ALU.is_ge, ALU.mult))
        a2 = p.op("dv", [a1], lambda v, sg=sg: v.tensor_scalar(
            sg, sg, 1.0, None, ALU.subtract))
        a3 = p.op("dv", [a2], lambda v, r_=r_, sg=sg: v.tensor_tensor(
            r_, r_, sg, ALU.mult))
        a4 = p.op("dv", [a3], lambda v, r_=r_: v.tensor_scalar(
            r_, r_, 0.25, None, ALU.max))
        a5 = p.op("dv", [a4], lambda v, r_=r_, sg=sg: v.tensor_tensor(
            r_, r_, sg, ALU.mult))
        t_clamp.append(a5)

    def emit_piece_bcast(cc3, mmlast):
        for n in range(2):
            colg = cc3 * WB + n * SUB
            piece, incol = colg // D, colg % D
            mslot_i = PIECE_SLOT[piece]
            plus1 = piece in (1, 4)
            ev = p.op("dv", [mmlast, mst["bc"]], lambda v, n=n: v.tensor_copy(
                mod_stage[0:1, 0:SUB], pb[4 + n][0:1, 0:SUB]))
            p.bank_done(ev, 4 + n)
            mm = p.op("pe", [ev, t_ones] + p.bank_dep(6),
                      lambda t: t.matmul(pb[6][:, 0:SUB], ones_f[:],
                                         mod_stage[0:1, 0:SUB],
                                         start=True, stop=True))
            mst["bc"] = mm
            if plus1:
                e2 = p.op("dv", [mm], lambda v, mslot_i=mslot_i, incol=incol:
                          v.tensor_scalar(
                              Mb[:, mslot_i * D + incol:mslot_i * D + incol + SUB],
                              pb[6][:, 0:SUB], 1.0, None, ALU.add))
            else:
                e2 = p.op("dv", [mm], lambda v, mslot_i=mslot_i, incol=incol:
                          v.tensor_copy(
                              Mb[:, mslot_i * D + incol:mslot_i * D + incol + SUB],
                              pb[6][:, 0:SUB]))
            p.bank_done(e2, 6)
            pieces_done[piece].append(e2)
        if cc3 in (2, 5):
            emit_clamp(cc3)

    def emit_mod_pair(cc3, j):
        idx = cc3 * MC + j
        slot = idx % NSLOT
        tw = p.dma("gp", f"q_wm{slot}", [wm_pe.get(idx - NSLOT)],
                   lambda g, cc3=cc3, j=j, slot=slot: g.dma_start(
                       wmod_b[:, slot * WB:slot * WB + WB],
                       wmod_in.ap()[j * 128:(j + 1) * 128,
                                    cc3 * WB:(cc3 + 1) * WB]))
        last = None
        for n in range(2):
            bdep = p.bank_dep(4 + n) if j == 0 else []
            last = p.op("pe", [tw, t_cts] + bdep,
                        lambda t, j=j, n=n, slot=slot: t.matmul(
                            pb[4 + n][0:1, 0:SUB], cts[:, j:j + 1],
                            wmod_b[:, slot * WB + n * SUB:
                                   slot * WB + (n + 1) * SUB],
                            start=(j == 0), stop=(j == MC - 1)))
        wm_pe[idx] = last
        if j == MC - 1:
            emit_piece_bcast(cc3, last)

    MOD_ILV = NCC
    for cc3 in range(MOD_ILV):
        for j in range(MC):
            emit_mod_pair(cc3, j)
    mod_rest = [(cc3, j) for cc3 in range(MOD_ILV, NCC) for j in range(MC)]

    wq_toks = [p.dma("gp", "q_wk", [], lambda g: g.dma_start(
        Wqb[:].rearrange("q (j d) -> q j d", j=MC),
        wq_in.ap().rearrange("(j q) d -> q j d", q=128)))]
    t_wkF = p.dma("gp", "q_wg0b", t_gwT, lambda g: g.dma_start(
        WkF[:].rearrange("q (j d) -> q j d", j=MC),
        wk_in.ap().rearrange("(j q) d -> q j d", q=128)))

    modA = pieces_done[0] + pieces_done[1]

    def ln_stats(xt_ap, dep, war_dep, par=0):
        o = 8 * par
        nl = len(LNC)
        tb = []
        for k, (c0, cw) in enumerate(LNC):
            tb.append(p.op("dv", [dep] + ([war_dep] if k == 0 else []),
                           lambda v, k=k, c0=c0, cw=cw: v.bn_stats(
                               bn6[:, (par * nl + k) * 6:(par * nl + k + 1) * 6],
                               xt_ap[:, c0:c0 + cw])))
        ta = p.op("dv", tb, lambda v: v.bn_aggr(
            st2[:, 2 * par:2 * par + 2],
            bn6[:, par * nl * 6:(par + 1) * nl * 6]))
        s1_ = p.op("ac", [ta, t_eps], lambda s: s.activation(
            scr[:, 1 + o:2 + o], st2[:, 2 * par + 1:2 * par + 2], AF.Sqrt,
            bias=scr[:, 0:1]))
        s2_ = p.op("dv", [s1_], lambda v: v.reciprocal(
            scr[:, 2 + o:3 + o], scr[:, 1 + o:2 + o]))
        s3_ = p.op("dv", [s2_], lambda v: v.tensor_scalar(
            scr[:, 3 + o:4 + o], st2[:, 2 * par:2 * par + 1],
            scr[:, 2 + o:3 + o], -1.0, ALU.mult, ALU.mult))
        return s3_, s1_

    xin_tok = [None] * NT
    xin_free = {}
    v_pe_last = None
    v_last = None
    hT_last = [None] * NT
    prev_ts1 = None
    hscr = [o_sb[:], hs1[:]]
    th_tok = [None] * NT
    pend_evac = None

    def emit_a1_evacs(i):
        par = i % 2
        hs = hscr[par]
        last = None
        for jg in range(0, MC, 4):
            ng = min(4, MC - jg)
            bank = 2 * par + (jg // 4) % 2
            tt = None
            for j_i in range(ng):
                tt = p.op("pe", [th_tok[i], t_idf] +
                          (p.bank_dep(bank) if j_i == 0 else []),
                          lambda t, j=jg + j_i, j_i=j_i, bank=bank, hs=hs:
                          t.transpose(
                              pb[bank][:, j_i * 128:(j_i + 1) * 128],
                              hs[:, j * 128:(j + 1) * 128], identf[:]))
            ev = p.op("ac", [tt], lambda v, jg=jg, ng=ng, i=i, bank=bank:
                      v.copy(
                          hT[:].rearrange("q (j s) -> q j s", j=MC)
                          [:, jg:jg + ng, i * 128:(i + 1) * 128],
                          pb[bank][:, 0:ng * 128]
                          .rearrange("q (j s) -> q j s", j=ng)))
            p.bank_done(ev, bank)
            hT_tr_done[par] = tt
            last = ev
        hT_last[i] = last
        for half in range(NDH):
            bank = 2 * par + half
            vmm = None
            for j in range(MC):
                vmm = p.op("pe", [wv_tok[half], last] +
                           (p.bank_dep(bank) if j == 0 else []),
                           lambda t, j=j, i=i, half=half, bank=bank: t.matmul(
                               pb[bank][:, 0:DHALF],
                               hT[:, j * S + i * 128:j * S + (i + 1) * 128],
                               Wkv[half][:, j * DHALF:(j + 1) * DHALF],
                               start=(j == 0), stop=(j == MC - 1)))
            ev = p.op("ac", [vmm], lambda v, i=i, half=half, bank=bank:
                      v.copy(
                          V[:, i * D + half * DHALF:i * D + (half + 1) * DHALF],
                          pb[bank][:, 0:DHALF]))
            p.bank_done(ev, bank)
            return_vals[0] = vmm
            return_vals[1] = ev

    return_vals = [None, None]
    hT_tr_done = [None, None]
    for i in range(NT):
        slot = i % 2
        par = i % 2
        hs = hscr[par]
        o = 8 * par
        dep = [xin_free.get(slot), xin_tok[i - 1] if i else None]
        td = p.dma("sp", "q_x", dep, lambda s, i=i: s.dma_start(
            xin[:, (i % 2) * D:((i % 2) + 1) * D],
            x_in.ap()[i * 128:(i + 1) * 128, :]))
        xin_tok[i] = td
        xt = xin[:, slot * D:(slot + 1) * D]
        s3_, prev_ts1 = ln_stats(xt, xin_tok[i], prev_ts1, par)
        tl = p.op("dv", [s3_], lambda v, xt=xt, hs=hs, o=o: v.tensor_scalar(
            hs, xt, scr[:, 2 + o:3 + o], scr[:, 3 + o:4 + o],
            ALU.mult, ALU.add))
        tm = p.op("dv", [tl] + modA, lambda v, hs=hs: v.tensor_tensor(
            hs, hs, mslot(0), ALU.mult))
        th = p.op("dv", [tm, hT_tr_done[par]], lambda v, hs=hs: v.tensor_tensor(
            hs, hs, mslot(1), ALU.add))
        th_tok[i] = th
        xin_free[slot] = th
        if pend_evac is not None:
            emit_a1_evacs(pend_evac)
        pend_evac = i
        for _ in range(2):
            if mod_rest:
                emit_mod_pair(*mod_rest.pop(0))
    emit_a1_evacs(pend_evac)
    while mod_rest:
        emit_mod_pair(*mod_rest.pop(0))
    mod_bc_final = mst["bc"]
    bc_ga = [t_clamp[0]]
    bc_gb = [t_clamp[1]]
    modB = pieces_done[3] + pieces_done[4]
    v_pe_last, v_last = return_vals
    hT_done = [hT_last[NT - 1]]

    kt_last = None
    kt_pe_last = None
    for half in range(NDH):
        for jd in range(DHALF // 128):
            d_chunk = half * (DHALF // 128) + jd
            for sb in range(S // 512):
                bank = sb % 2
                last = None
                for j in range(MC):
                    last = p.op("pe", [t_wkF] + hT_done +
                                (p.bank_dep(bank) if j == 0 else []),
                                lambda t, j=j, jd=jd, sb=sb, bank=bank, half=half:
                                t.matmul(
                                    pb[bank][:, :],
                                    WkF[:, j * D + half * DHALF + jd * 128:
                                        j * D + half * DHALF + (jd + 1) * 128],
                                    hT[:, j * S + sb * 512:j * S + (sb + 1) * 512],
                                    start=(j == 0), stop=(j == MC - 1)))
                kt_pe_last = last
                ev = p.op("dv", [last] +
                          ([wm_pe[NCC * MC - 1]] if half == 1 else []),
                          lambda v, d_chunk=d_chunk, sb=sb, bank=bank:
                          v.tensor_copy(
                              KT[:, d_chunk * S + sb * 512:d_chunk * S + (sb + 1) * 512],
                              pb[bank][:, :]))
                p.bank_done(ev, bank)
                kt_last = ev

    if dbg:
        p.dma("sp", "q_pst", [v_last], lambda s: s.dma_start(dbg_V.ap(), V[:]))

    xi2_tok = [None] * NT
    xi2_free = {}
    spill_tok = {}
    pt_tr_last = [None] * 4
    exp_last_par = [None] * 4
    AC = {0: dict(cm=8, acc=12, rm=4, nm=5, rs=6, ri=7),
          1: dict(cm=16, acc=20, rm=24, nm=25, rs=26, ri=27),
          2: dict(cm=28, acc=32, rm=36, nm=37, rs=38, ri=39),
          3: dict(cm=40, acc=44, rm=48, nm=49, rs=50, ri=51)}
    pend = None
    qt_pe_final = [None]
    scores_final = [None]
    x1_tok = [None] * NT

    def emit_ptav(ctx):
        qi, exp_toks, nb = ctx
        par = qi % 4
        C = AC[par]
        slot = qi % 2
        Ptb = Pts[par]
        palt = (2 * qi) % 6
        ngr = (qi + 1 + 3) // 4
        evs = []
        tr_last = None
        for g4 in range(ngr):
            bank = 6 if g4 % 2 == 0 else palt
            kts = list(range(g4 * 4, min(g4 * 4 + 4, qi + 1)))
            for k_i, kt in enumerate(kts):
                tr_last = p.op("pe",
                               [exp_toks[min(kt * 128 // 512, nb - 1)], t_idb] +
                               (p.bank_dep(bank) if k_i == 0 else []),
                               lambda t, kt=kt, k_i=k_i, bank=bank: t.transpose(
                                   pbb[bank][:, k_i * 128:(k_i + 1) * 128],
                                   Ptb[:, kt * 128:(kt + 1) * 128], identb[:]))
            nk = len(kts)
            ev = p.op("dv", [tr_last], lambda v, g4=g4, nk=nk, bank=bank:
                      v.tensor_copy(PT[:, g4 * 512:g4 * 512 + nk * 128],
                                    pbb[bank][:, 0:nk * 128]))
            p.bank_done(ev, bank)
            evs.append(ev)
        pt_tr_last[par] = tr_last
        tosb = []
        for mh, (m0, mw) in enumerate(MH):
            bank_av = 7 if mh == 0 else 6
            av_last = None
            for kt in range(qi + 1):
                dep = [evs[kt // 4]] if kt % 4 == 0 else []
                if kt == 0:
                    dep = dep + [v_last] + p.bank_dep(bank_av)
                av_last = p.op("pe", dep,
                               lambda t, kt=kt, qi=qi, m0=m0, mw=mw,
                               bank_av=bank_av: t.matmul(
                                   pb[bank_av][:, 0:mw],
                                   PT[:, kt * 128:(kt + 1) * 128],
                                   V[:, kt * D + m0:kt * D + m0 + mw],
                                   start=(kt == 0), stop=(kt == qi)))
            ts_ = p.op("ac", [av_last], lambda s, m0=m0, mw=mw, C=C,
                       bank_av=bank_av: s.mul(
                           o_sb[:, m0:m0 + mw], pb[bank_av][:, 0:mw],
                           scr[:, C["ri"]:C["ri"] + 1]))
            p.bank_done(ts_, bank_av)
            tosb.append(ts_)
        tga = p.op("dv", tosb + bc_ga, lambda v: v.tensor_tensor(
            o_sb[:], o_sb[:], mslot(2), ALU.mult))
        if dbg:
            dbg_tok[0] = p.dma("sp", "q_pst", [tga, dbg_tok[0]],
                               lambda s, qi=qi: s.dma_start(
                                   dbg_o.ap()[qi * 128:(qi + 1) * 128, :],
                                   o_sb[:]))
        txa = p.op("dv", [tga, xi2_tok[qi], spill_tok.get(qi - 1)] +
                   (dbg_tok if dbg else []),
                   lambda v, slot=slot:
                   v.tensor_tensor(xin[:, slot * D:(slot + 1) * D], o_sb[:],
                                   xin[:, slot * D:(slot + 1) * D], ALU.add))
        xi2_free[slot] = txa
        spill_tok[qi] = p.dma("sp", "q_x", [txa], lambda s, qi=qi, slot=slot:
                              s.dma_start(
                                  x1_d.ap()[qi * 128:(qi + 1) * 128, :],
                                  xin[:, slot * D:(slot + 1) * D]))

    for qi in range(NT):
        slot = qi % 2
        par = qi % 4
        C = AC[par]
        dep = [xi2_free.get(slot) if qi >= 2 else xin_free.get(slot),
               spill_tok.get(qi - 2),
               xi2_tok[qi - 1] if qi else xin_tok[NT - 1]]
        td = p.dma("sp", "q_x", dep, lambda s, qi=qi, slot=slot: s.dma_start(
            xin[:, slot * D:(slot + 1) * D],
            x_in.ap()[qi * 128:(qi + 1) * 128, :]))
        xi2_tok[qi] = td
        keys = (qi + 1) * 128
        kblocks = splits(keys, 512)
        nb = len(kblocks)
        pool = [(2 * qi + kb) % 6 for kb in range(nb)]
        if qi % 4 == 0:
            b4 = qi // 4
            qt_ev = []
            for jd in range(MC):
                bank = pool[jd % min(2, nb)]
                last = None
                for j in range(MC):
                    last = p.op("pe", wq_toks + hT_done +
                                (p.bank_dep(bank) if j == 0 else []),
                                lambda t, j=j, jd=jd, b4=b4, bank=bank: t.matmul(
                                    pb[bank][:, :],
                                    Wqb[:, j * D + jd * 128:j * D + (jd + 1) * 128],
                                    hT[:, j * S + b4 * 512:j * S + (b4 + 1) * 512],
                                    start=(j == 0), stop=(j == MC - 1)))
                ev = p.op("dv", [last] +
                          ([mod_bc_final] if b4 == 0 and jd == 0 else []),
                          lambda v, jd=jd, bank=bank: v.tensor_copy(
                              qt4[:, jd * 512:(jd + 1) * 512], pb[bank][:, :]))
                p.bank_done(ev, bank)
                qt_ev.append(ev)
            qt_pe_final[0] = last
        blk_pe = [None] * nb
        for kb in list(range(2, nb)) + list(range(min(2, nb))):
            k0, kw = kblocks[kb]
            bank = pool[kb]
            last = None
            for jd in range(MC):
                last = p.op("pe", qt_ev + [kt_last] +
                            (p.bank_dep(bank) if jd == 0 else []),
                            lambda t, jd=jd, k0=k0, kw=kw, bank=bank, qi=qi:
                            t.matmul(
                                pb[bank][:, 0:kw],
                                qt4[:, jd * 512 + (qi % 4) * 128:
                                    jd * 512 + (qi % 4) * 128 + 128],
                                KT[:, jd * S + k0:jd * S + k0 + kw],
                                start=(jd == 0), stop=(jd == MC - 1)))
            blk_pe[kb] = last
        if qi == NT - 1:
            scores_final[0] = blk_pe[-1]
        lw = kblocks[-1][1]
        lbank = pool[nb - 1]
        t_mask = p.op("dv", [blk_pe[-1], t_dm], lambda v, lbank=lbank, lw=lw:
                      v.scalar_tensor_tensor(pb[lbank][:, lw - 128:lw],
                                             strictU[:], float(NEG),
                                             pb[lbank][:, lw - 128:lw],
                                             ALU.mult, ALU.add))
        cm_toks = []
        for kb, (k0, kw) in enumerate(kblocks):
            dep = [t_mask] if kb == nb - 1 else [blk_pe[kb]]
            cm_toks.append(p.op("dv", dep, lambda v, kb=kb, kw=kw, C=C,
                                bank=pool[kb]: v.reduce_max(
                                    scr[:, C["cm"] + kb:C["cm"] + kb + 1],
                                    pb[bank][:, 0:kw], axis=AX.X)))
        t_rm = p.op("dv", cm_toks, lambda v, nb=nb, C=C: v.reduce_max(
            scr[:, C["rm"]:C["rm"] + 1], scr[:, C["cm"]:C["cm"] + nb], axis=AX.X))
        t_nm = p.op("dv", [t_rm, exp_last_par[par]], lambda v, C=C: v.tensor_scalar(
            scr[:, C["nm"]:C["nm"] + 1], scr[:, C["rm"]:C["rm"] + 1],
            -ISQ, None, ALU.mult))
        exp_toks = []
        for kb, (k0, kw) in enumerate(kblocks):
            ex = p.op("ac", [t_nm, pt_tr_last[par]],
                      lambda s, kb=kb, k0=k0, kw=kw, C=C, par=par,
                      bank=pool[kb]: s.activation(
                          Pts[par][:, k0:k0 + kw], pb[bank][:, 0:kw], AF.Exp,
                          bias=scr[:, C["nm"]:C["nm"] + 1], scale=ISQ,
                          accum_out=scr[:, C["acc"] + kb:C["acc"] + kb + 1]))
            p.bank_done(ex, pool[kb])
            exp_toks.append(ex)
        exp_last_par[par] = exp_toks[-1]
        if dbg and qi == 1:
            p.dma("sp", "q_pst", exp_toks, lambda s, keys=keys: s.dma_start(
                dbg_P.ap()[:, 0:keys], Pts[1][:, 0:keys]))
            p.dma("sp", "q_pst", exp_toks + qt_ev + [t_nm],
                  lambda s, qi=qi: s.dma_start(
                      dbg_q.ap(),
                      qt4[:].rearrange("q (jd four d) -> q jd four d",
                                       four=4, d=128)[:, :, qi % 4]))
            p.dma("sp", "q_pst", exp_toks + [t_nm],
                  lambda s: s.dma_start(dbg_nm.ap(), scr[:, 24:26]))
        t_rs = p.op("dv", exp_toks, lambda v, nb=nb, C=C: v.tensor_reduce(
            scr[:, C["rs"]:C["rs"] + 1], scr[:, C["acc"]:C["acc"] + nb],
            axis=AX.X, op=ALU.add))
        p.op("dv", [t_rs], lambda v, C=C: v.reciprocal(
            scr[:, C["ri"]:C["ri"] + 1], scr[:, C["rs"]:C["rs"] + 1]))
        if pend is not None:
            emit_ptav(pend)
        pend = (qi, exp_toks, nb)
        if qi == 13:
            for b4 in range(2):
                td_r = p.dma("sp", "q_x",
                             [qt_pe_final[0]] +
                             [spill_tok[b4 * 4 + k] for k in range(4)],
                             lambda s, b4=b4: s.dma_start(
                                 x1[:, b4 * 4 * D:(b4 + 1) * 4 * D],
                                 x1_d.ap()[b4 * 512:(b4 + 1) * 512, :]
                                 .rearrange("(a q) d -> q a d", q=128)))
                for i_ in range(4):
                    x1_tok[b4 * 4 + i_] = td_r
    emit_ptav(pend)
    attn_done = spill_tok[NT - 1]

    out_add = {i: None for i in range(NT)}
    gctr = [0]
    wg_tok = [None, None]
    wg_free = [None, None]

    def load_group(e, g, extra=()):
        buf = gctr[0] % 2
        gctr[0] += 1
        deps = [wg_free[buf]] + list(extra)
        d1 = p.dma("gp", f"q_wg{buf}a", deps, lambda gp, e=e, g=g, buf=buf:
                   gp.dma_start(wgbuf[buf][0][:],
                                w1_in.ap()[e, :, g * 512:(g + 1) * 512]
                                .rearrange("(j q) f -> q j f", q=128)))
        d3 = p.dma("gp", f"q_wg{buf}b", deps, lambda gp, e=e, g=g, buf=buf:
                   gp.dma_start(wgbuf[buf][1][:],
                                w3_in.ap()[e, :, g * 512:(g + 1) * 512]
                                .rearrange("(j q) f -> q j f", q=128)))
        wg_tok[buf] = (d1, d3)
        return buf

    load_group(0, 0, extra=[qt_pe_final[0]])
    if FG > 1:
        load_group(0, 1, extra=[v_pe_last, pt_tr_last[2], pt_tr_last[3]])

    for b4 in range(2, NT // 4):
        td_r = p.dma("sp", "q_x",
                     [scores_final[0]] +
                     [spill_tok[b4 * 4 + k] for k in range(4)],
                     lambda s, b4=b4: s.dma_start(
                         x1[:, b4 * 4 * D:(b4 + 1) * 4 * D],
                         x1_d.ap()[b4 * 512:(b4 + 1) * 512, :]
                         .rearrange("(a q) d -> q a d", q=128)))
        for i_ in range(4):
            x1_tok[b4 * 4 + i_] = td_r

    lg_tok = []
    rt_toks = []
    h2_done = None
    prev_ts1 = None
    router_last = {}
    tcast_tok = [None] * NT
    pend_lg = None

    def emit_routing4(g4, evs_lg):
        j0 = 4 * g4
        tiles = list(range(j0, j0 + 4))
        sl4 = lg_a[:, j0 * E:(j0 + 4) * E]
        t1 = p.op("dv", evs_lg, lambda v, j0=j0, sl4=sl4: v.reduce_max(
            scr[:, 16 + j0:20 + j0],
            sl4.rearrange("q (t e) -> q t e", e=E), axis=AX.X))
        p1s = []
        for j in tiles:
            p1s.append(p.op("dv", [t1], lambda v, j=j: v.tensor_scalar(
                probs_a[:, j * E:(j + 1) * E], lg_a[:, j * E:(j + 1) * E],
                scr[:, 16 + j:17 + j], None, ALU.subtract)))
        e1 = p.op("ac", p1s, lambda s, j0=j0: s.activation(
            probs_a[:, j0 * E:(j0 + 4) * E], probs_a[:, j0 * E:(j0 + 4) * E],
            AF.Exp))
        t2 = p.op("dv", [e1], lambda v, j0=j0: v.tensor_reduce(
            scr[:, 32 + j0:36 + j0],
            probs_a[:, j0 * E:(j0 + 4) * E].rearrange("q (t e) -> q t e", e=E),
            axis=AX.X, op=ALU.add))
        r_ = p.op("dv", [t2], lambda v, j0=j0: v.reciprocal(
            scr[:, 32 + j0:36 + j0], scr[:, 32 + j0:36 + j0]))
        last_n = None
        for j in tiles:
            last_n = p.op("dv", [r_], lambda v, j=j: v.tensor_scalar(
                probs_a[:, j * E:(j + 1) * E], probs_a[:, j * E:(j + 1) * E],
                scr[:, 32 + j:33 + j], None, ALU.mult))
        pbc = p.op("dv", [last_n], lambda v, j0=j0: v.tensor_copy(
            probs_b[:, j0 * E:(j0 + 4) * E], probs_a[:, j0 * E:(j0 + 4) * E]))
        ev = None
        for j in tiles:
            a_ = p.op("dv", [t1], lambda v, j=j: v.tensor_scalar(
                mk_a[:, j * E:(j + 1) * E], lg_a[:, j * E:(j + 1) * E],
                scr[:, 16 + j:17 + j], None, ALU.is_ge))
            b_ = p.op("dv", [a_], lambda v, j=j: v.scalar_tensor_tensor(
                pos_a[:, j * E:(j + 1) * E], mk_a[:, j * E:(j + 1) * E],
                float(NEG), lg_a[:, j * E:(j + 1) * E], ALU.mult, ALU.add))
            c_ = p.op("dv", [b_], lambda v, j=j: v.reduce_max(
                scr[:, 16 + j:17 + j], pos_a[:, j * E:(j + 1) * E], axis=AX.X))
            d_ = p.op("dv", [c_], lambda v, j=j: v.tensor_scalar(
                pos_a[:, j * E:(j + 1) * E], pos_a[:, j * E:(j + 1) * E],
                scr[:, 16 + j:17 + j], None, ALU.is_ge))
            m_ = p.op("dv", [d_], lambda v, j=j: v.tensor_tensor(
                mk_a[:, j * E:(j + 1) * E], mk_a[:, j * E:(j + 1) * E],
                pos_a[:, j * E:(j + 1) * E], ALU.add))
            tt = p.op("pe", [m_, t_idf] + p.bank_dep(6),
                      lambda t, j=j: t.transpose(
                          pb[6][0:E, 0:128], mk_a[:, j * E:(j + 1) * E],
                          identf[:]))
            ev = p.op("dv", [tt], lambda v, j=j: v.tensor_copy(
                mkT_f[:, j * 128:(j + 1) * 128], pb[6][0:E, 0:128]))
            p.bank_done(ev, 6)
        return pbc, ev
    for i in range(NT):
        par = i % 2
        h2f = h2f_scr[par]
        h2T = h2T_scr[par]
        xt = x1[:, i * D:(i + 1) * D]
        s3_, prev_ts1 = ln_stats(xt, x1_tok[i], prev_ts1, par)
        tl = p.op("dv", [s3_, kt_pe_last, router_last.get(i - 2),
                         tcast_tok[i - 2] if i >= 2 else None],
                  lambda v, xt=xt, h2f=h2f, par=par:
                  v.tensor_scalar(h2f[:], xt, scr[:, 2 + 8 * par:3 + 8 * par],
                                  scr[:, 3 + 8 * par:4 + 8 * par],
                                  ALU.mult, ALU.add))
        tm = p.op("gp", [tl] + modB, lambda g, h2f=h2f: g.tensor_tensor(
            h2f[:], h2f[:], mslot(3), ALU.mult))
        th = p.op("dv", [tm], lambda v, h2f=h2f: v.tensor_tensor(
            h2f[:], h2f[:], mslot(4), ALU.add))
        tcast = p.op("ac", [th, attn_done], lambda v, i=i, h2f=h2f: v.copy(
            h2b[:, i * D:(i + 1) * D], h2f[:]))
        tcast_tok[i] = tcast
        h2_done = tcast
        trs = []
        for jg in range(0, MC, 4):
            ng = min(4, MC - jg)
            bank = 2 * par + (jg // 4) % 2
            tt = None
            for j_i in range(ng):
                tt = p.op("pe", [th, t_idf] +
                          (p.bank_dep(bank) if j_i == 0 else []),
                          lambda t, j=jg + j_i, j_i=j_i, bank=bank, h2f=h2f:
                          t.transpose(
                              pb[bank][:, j_i * 128:(j_i + 1) * 128],
                              h2f[:, j * 128:(j + 1) * 128], identf[:]))
            ev = p.op("ac", [tt], lambda s, jg=jg, ng=ng, bank=bank, h2T=h2T:
                      s.copy(h2T[:, jg * 128:(jg + ng) * 128],
                             pb[bank][:, 0:ng * 128]))
            p.bank_done(ev, bank)
            trs.append(ev)
        last = None
        for j in range(MC):
            last = p.op("pe", [trs[j // 4]] + (t_gwT if i == 0 else []) +
                        (p.bank_dep(4 + par) if j == 0 else []),
                        lambda t, j=j, par=par, h2T=h2T: t.matmul(
                            pb[4 + par][:, 0:E],
                            h2T[:, j * 128:(j + 1) * 128],
                            gwT[:, j * E:(j + 1) * E],
                            start=(j == 0), stop=(j == MC - 1)))
        router_last[i] = last
        if pend_lg is not None:
            j, jpar = pend_lg
            ev = p.op("dv", [router_last[j]], lambda v, j=j, jpar=jpar:
                      v.tensor_copy(lg_a[:, j * E:(j + 1) * E],
                                    pb[4 + jpar][:, 0:E]))
            p.bank_done(ev, 4 + jpar)
            lg_tok.append(ev)
            if len(lg_tok) % 4 == 0:
                rt_toks.append(emit_routing4(len(lg_tok) // 4 - 1,
                                             lg_tok[-4:]))
        pend_lg = (i, par)
    j, jpar = pend_lg
    ev = p.op("dv", [router_last[j]], lambda v, j=j, jpar=jpar: v.tensor_copy(
        lg_a[:, j * E:(j + 1) * E], pb[4 + jpar][:, 0:E]))
    p.bank_done(ev, 4 + jpar)
    lg_tok.append(ev)
    rt_toks.append(emit_routing4(len(lg_tok) // 4 - 1, lg_tok[-4:]))
    norm_last = rt_toks[-1][0]

    t_scan = p.op("dv", [rt_toks[-1][1]], lambda v: v.tensor_tensor_scan(
        mkT_f[:], mkT_f[:], mkT_f[:], 0.0, ALU.add, ALU.bypass))
    C_last = None
    for i in range(NT):
        tt = p.op("pe", [t_scan, t_idf] + p.bank_dep(3),
                  lambda t, i=i: t.transpose(
                      pb[3][:, 0:E], mkT_f[0:E, i * 128:(i + 1) * 128],
                      identf[0:E, 0:E]))
        C_last = p.op("dv", [tt], lambda v, i=i: v.tensor_copy(
            C_a[:, i * E:(i + 1) * E], pb[3][:, 0:E]))
        p.bank_done(C_last, 3)
    C_last = p.op("dv", [C_last], lambda v: v.tensor_tensor(
        C_a[:], C_a[:], mk_a[:], ALU.subtract))
    tv1 = p.op("dv", [C_last], lambda v: v.tensor_scalar(
        pos_a[:], C_a[:], float(CAP), None, ALU.is_lt))
    tv2 = p.op("dv", [tv1], lambda v: v.tensor_tensor(
        pos_a[:], pos_a[:], mk_a[:], ALU.mult))
    tv3 = p.op("dv", [tv2], lambda v: v.tensor_scalar(
        C_a[:], C_a[:], 1.0, None, ALU.add))
    tv4 = p.op("dv", [tv3], lambda v: v.tensor_tensor(
        pos_a[:], pos_a[:], C_a[:], ALU.mult))
    tpos = p.op("dv", [tv4], lambda v: v.tensor_scalar(
        pos_a[:], pos_a[:], 1.0, None, ALU.subtract))
    posT_last = None
    for i in range(NT):
        tt = p.op("pe", [tpos, t_idf] + p.bank_dep(3),
                  lambda t, i=i: t.transpose(
                      pb[3][0:E, 0:128], pos_a[:, i * E:(i + 1) * E], identf[:]))
        posT_last = p.op("dv", [tt], lambda v, i=i: v.tensor_copy(
            posT[:, i * 128:(i + 1) * 128], pb[3][0:E, 0:128]))
        p.bank_done(posT_last, 3)

    w2_free = [None, None]

    def load_w2(e, g, half, lastbuf, deps):
        w1g, w3g = wgbuf[lastbuf]
        dst = w1g if half == 0 else w3g
        tok = p.dma("gp", f"q_w2{half}", [w2_free[half]] + deps,
                    lambda gp, e=e, g=g, dst=dst: gp.dma_start(
                        dst[:].rearrange("q (fj m) -> q fj m", fj=4),
                        w2_in.ap()[e, g * 512:(g + 1) * 512, :]
                        .rearrange("(fj q) m -> q fj m", q=128)))
        return tok, dst

    de_free = None
    combine_pe_last = None
    sel_free = [None]
    w2s_war = [None]

    first_expert_deps = [attn_done, h2_done, tpos, norm_last]

    for e in range(E):
        disp_ev_last = None
        ps_row_ev = []
        de_pe_last = None
        for chalf in range(2):
            de_toks = []
            for i in range(NT):
                dep = [tpos, t_iotac]
                if i == 0:
                    dep += [de_pe_last if chalf == 1 else de_free]
                    if chalf == 0:
                        dep += [w2s_war[0]]
                de_toks.append(p.op("dv", dep,
                                    lambda v, i=i, e=e, chalf=chalf:
                                    v.tensor_scalar(
                                        De[:, i * CH:(i + 1) * CH],
                                        iotac[:, 0:CH],
                                        pos_a[:, i * E + e:i * E + e + 1],
                                        float(-chalf * CH),
                                        ALU.subtract, ALU.is_equal)))
            for j in range(MC):
                bank = j % 2
                last = None
                for i in range(NT):
                    dep = [de_toks[i]]
                    if i == 0:
                        dep += p.bank_dep(bank)
                        if j == 0 and chalf == 0:
                            dep += first_expert_deps if e == 0 else [combine_pe_last]
                    last = p.op("pe", dep, lambda t, i=i, j=j, bank=bank: t.matmul(
                        pb[bank][:, 0:CH],
                        h2b[:, i * D + j * 128:i * D + (j + 1) * 128],
                        De[:, i * CH:(i + 1) * CH],
                        start=(i == 0), stop=(i == NT - 1)))
                ev = p.op("dv", [last], lambda v, j=j, chalf=chalf, bank=bank:
                          v.tensor_copy(
                              disp[:, j * CAP + chalf * CH:j * CAP + (chalf + 1) * CH],
                              pb[bank][:, 0:CH]))
                p.bank_done(ev, bank)
                disp_ev_last = ev
            last = None
            for i in range(NT):
                dep = [de_toks[i]]
                if i == 0:
                    dep += p.bank_dep(2)
                last = p.op("pe", dep, lambda t, i=i, e=e: t.matmul(
                    pb[2][0:1, 0:CH], probs_b[:, i * E + e:i * E + e + 1],
                    De[:, i * CH:(i + 1) * CH],
                    start=(i == 0), stop=(i == NT - 1)))
            de_pe_last = last
            ev = p.op("dv", [last], lambda v, chalf=chalf: v.tensor_copy(
                prob_row[0:1, chalf * CH:(chalf + 1) * CH], pb[2][0:1, 0:CH]))
            p.bank_done(ev, 2)
            ps_row_ev.append(ev)
        de_free = de_pe_last
        t_sel = p.op("dv", [t_iotaP, posT_last, sel_free[0]],
                     lambda v, e=e: v.tensor_scalar(
                         sel[:], iotaP[:], float(e), None, ALU.is_equal))

        def emit_cm_half(sh):
            shs = S // SH
            out = {}
            for sc, (sc0, scw) in enumerate(splits(shs, 512)):
                scg = sh * shs + sc0
                bank = 6 + sc
                mm = p.op("pe", [t_sel] + p.bank_dep(bank),
                          lambda t, scg=scg, scw=scw, bank=bank: t.matmul(
                              pb[bank][:, 0:scw], sel[:],
                              posT[0:E, scg:scg + scw],
                              start=True, stop=True))
                sel_free[0] = mm
                cm_last = None
                for cchunk in range(CC):
                    wdt = min(128, CAP - cchunk * 128)
                    cm_last = p.op("dv", [mm, t_ccol],
                                   lambda v, cchunk=cchunk, sc0=sc0, scw=scw,
                                   wdt=wdt, bank=bank: v.tensor_scalar(
                                       CM[0:wdt, cchunk * shs + sc0:
                                          cchunk * shs + sc0 + scw],
                                       pb[bank][0:wdt, 0:scw],
                                       ccol[0:wdt, cchunk:cchunk + 1],
                                       None, ALU.is_equal))
                p.bank_done(cm_last, bank)
                out[(sh, sc)] = cm_last
            return out

        cm_blocks = {}
        pcol_toks = []
        for cchunk in range(CC):
            tt = p.op("pe", ps_row_ev + [t_onesb] + p.bank_dep(3),
                      lambda t, cchunk=cchunk: t.matmul(
                          pb[3][:, 0:1],
                          prob_row[0:1, cchunk * 128:(cchunk + 1) * 128],
                          ones_b[0:1, 0:1], start=True, stop=True))
            ev = p.op("dv", [tt], lambda v, cchunk=cchunk: v.tensor_copy(
                scr[:, 20 + cchunk:21 + cchunk], pb[3][:, 0:1]))
            p.bank_done(ev, 3)
            pcol_toks.append(ev)

        g_last = None
        h_pe_last = None
        for g in range(FG):
            cur = (e * FG + g) % 2
            tw1, tw3 = wg_tok[cur]
            w1g, w3g = wgbuf[cur]
            for fi in range(4):
                fc = g * 4 + fi
                par = fc % 2
                b1 = 0 if par == 0 else 4
                b3 = 2 if par == 0 else 6
                h1_last = None
                h3_last = None
                for j in range(MC):
                    for cs, (c0, cw) in enumerate(CS):
                        dep = [tw1]
                        if j == 0:
                            dep += p.bank_dep(b1 + cs)
                            if cs == 0:
                                dep += [disp_ev_last]
                        h1_last = p.op("pe", dep,
                                       lambda t, j=j, fi=fi, c0=c0, cw=cw, b1=b1,
                                       cs=cs, w1g=w1g: t.matmul(
                                           pb[b1 + cs][:, 0:cw],
                                           w1g[:, j * 512 + fi * 128:j * 512 + (fi + 1) * 128],
                                           disp[:, j * CAP + c0:j * CAP + c0 + cw],
                                           start=(j == 0), stop=(j == MC - 1)))
                for j in range(MC):
                    for cs, (c0, cw) in enumerate(CS):
                        dep = [tw3]
                        if j == 0:
                            dep += p.bank_dep(b3 + cs)
                        h3_last = p.op("pe", dep,
                                       lambda t, j=j, fi=fi, c0=c0, cw=cw, b3=b3,
                                       cs=cs, w3g=w3g: t.matmul(
                                           pb[b3 + cs][:, 0:cw],
                                           w3g[:, j * 512 + fi * 128:j * 512 + (fi + 1) * 128],
                                           disp[:, j * CAP + c0:j * CAP + c0 + cw],
                                           start=(j == 0), stop=(j == MC - 1)))
                h_pe_last = h3_last
                sl = []
                for cs, (c0, cw) in enumerate(CS):
                    sl.append(p.op("ac", [h1_last, g_last], lambda s, c0=c0,
                                   cw=cw, b1=b1, cs=cs: s.activation(
                                       s1[:, c0:c0 + cw], pb[b1 + cs][:, 0:cw],
                                       AF.Silu)))
                for cs in range(len(CS)):
                    p.bank_done(sl[cs], b1 + cs)
                gt = None
                for cs, (c0, cw) in enumerate(CS):
                    gt = p.op("dv", [sl[cs], h3_last],
                              lambda v, fc=fc, c0=c0, cw=cw, b3=b3, cs=cs:
                              v.tensor_tensor(
                                  gT[:, fc * CAP + c0:fc * CAP + c0 + cw],
                                  s1[:, c0:c0 + cw], pb[b3 + cs][:, 0:cw],
                                  ALU.mult))
                    p.bank_done(gt, b3 + cs)
                g_last = gt
            wg_free[cur] = h3_last
            if g + 2 < FG:
                load_group(e, g + 2)

        lastbuf = (e * FG + FG - 1) % 2
        otherbuf = 1 - lastbuf
        cgroups = []
        ci = 0
        while ci < CC:
            cgroups.append(list(range(ci, min(ci + 3, CC))))
            ci += 3
        ring = [w2s, wgbuf[lastbuf][0], wgbuf[lastbuf][1],
                wgbuf[otherbuf][0], wgbuf[otherbuf][1]]
        ring_free = [w2s_war[0], wg_free[lastbuf], wg_free[lastbuf],
                     wg_free[otherbuf], wg_free[otherbuf]]
        ring_free[0] = de_free if ring_free[0] is None else ring_free[0]
        oe_ev_last = None
        w2_pe_last = None
        w2idx = 0
        for gi, cg in enumerate(cgroups):
            mm_last = {}
            for g in range(FG):
                slot = w2idx % 5
                w2idx += 1
                tok = p.dma("gp", f"q_w2{slot}", [ring_free[slot]] +
                            ([de_free] if slot == 0 else []),
                            lambda gp, e=e, g=g, dst=ring[slot]: gp.dma_start(
                                dst[:].rearrange("q (fj m) -> q fj m", fj=4),
                                w2_in.ap()[e, g * 512:(g + 1) * 512, :]
                                .rearrange("(fj q) m -> q fj m", q=128)))
                w2t = ring[slot]
                for fi in range(4):
                    fc = g * 4 + fi
                    for ti, ct_ in enumerate(cg):
                        wdt = min(128, CAP - ct_ * 128)
                        for mh, (m0, mw) in enumerate(MH):
                            bank = 2 * ti + mh if len(MH) > 1 else ti
                            dep = [tok]
                            if fc == 0:
                                dep += [g_last]
                                dep += p.bank_dep(bank)
                            mm = p.op("pe", dep,
                                      lambda t, fc=fc, fi=fi, ti=ti, ct_=ct_,
                                      wdt=wdt, m0=m0, mw=mw, bank=bank, w2t=w2t:
                                      t.matmul(
                                          pb[bank][:, 0:mw],
                                          gT[:, fc * CAP + ct_ * 128:fc * CAP + ct_ * 128 + wdt],
                                          w2t[:, fi * D + m0:fi * D + m0 + mw],
                                          start=(fc == 0), stop=(fc == FC - 1)))
                            mm_last[(ti, mh)] = mm
                            w2_pe_last = mm
                ring_free[slot] = w2_pe_last
                if slot == 0:
                    w2s_war[0] = w2_pe_last
                if gi == len(cgroups) - 1 and g == 1:
                    cm_blocks.update(emit_cm_half(0))
            for ti, ct_ in enumerate(cg):
                wdt = min(128, CAP - ct_ * 128)
                for mh, (m0, mw) in enumerate(MH):
                    bank = 2 * ti + mh if len(MH) > 1 else ti
                    e1 = p.op("dv", [mm_last[(ti, mh)], pcol_toks[ct_]] +
                              ([h_pe_last] if gi == 0 and ti == 0 and mh == 0 else []),
                              lambda v, ct_=ct_, wdt=wdt, m0=m0, mw=mw, bank=bank:
                              v.tensor_scalar(
                                  out_e[0:wdt, ct_ * D + m0:ct_ * D + m0 + mw],
                                  pb[bank][0:wdt, 0:mw],
                                  scr[0:wdt, 20 + ct_:21 + ct_], None, ALU.mult))
                    e2 = p.op("dv", [e1] + bc_gb,
                              lambda v, ct_=ct_, wdt=wdt, m0=m0, mw=mw:
                              v.tensor_tensor(
                                  out_e[0:wdt, ct_ * D + m0:ct_ * D + m0 + mw],
                                  out_e[0:wdt, ct_ * D + m0:ct_ * D + m0 + mw],
                                  Mb[0:wdt, 5 * D + m0:5 * D + m0 + mw],
                                  ALU.mult))
                    p.bank_done(e2, bank)
                    oe_ev_last = e2
        if e + 1 < E:
            slot_of = {lastbuf: (1, 2), otherbuf: (3, 4)}
            b0 = ((e + 1) * FG) % 2
            wg_free[b0] = ring_free[slot_of[b0][0]]
            load_group(e + 1, 0, extra=[ring_free[slot_of[b0][1]]])
            if FG > 1:
                b1 = ((e + 1) * FG + 1) % 2
                wg_free[b1] = ring_free[slot_of[b1][0]]
                load_group(e + 1, 1, extra=[ring_free[slot_of[b1][1]]])

        for sh in range(SH):
            shs = S // SH
            if sh > 0:
                cm_blocks.update(emit_cm_half(sh))
            for i in range(sh * (NT // SH), (sh + 1) * (NT // SH)):
                i_in = i - sh * (NT // SH)
                bank = 4 + 2 * (i % 2) if len(MH) > 1 else 4 + (i % 2)
                last = None
                for cchunk in range(CC):
                    wdt = min(128, CAP - cchunk * 128)
                    for mh, (m0, mw) in enumerate(MH):
                        dep = []
                        if cchunk == 0 and mh == 0:
                            dep = [cm_blocks[(sh, i_in * 128 // 512)]]
                            for mh2 in range(len(MH)):
                                dep += p.bank_dep(bank + mh2)
                        last = p.op("pe", dep,
                                    lambda t, i_in=i_in, cchunk=cchunk, m0=m0,
                                    mw=mw, mh=mh, wdt=wdt, bank=bank: t.matmul(
                                        pb[bank + mh][:, 0:mw],
                                        CM[0:wdt, cchunk * shs + i_in * 128:
                                           cchunk * shs + (i_in + 1) * 128],
                                        out_e[0:wdt, cchunk * D + m0:
                                              cchunk * D + m0 + mw],
                                        start=(cchunk == 0), stop=(cchunk == CC - 1)))
                combine_pe_last = last
                for mh, (m0, mw) in enumerate(MH):
                    ta_ = p.op("dv", [last, out_add[i]],
                               lambda v, i=i, m0=m0, mw=mw, mh=mh, bank=bank:
                               v.tensor_tensor(
                                   x1[:, i * D + m0:i * D + m0 + mw],
                                   x1[:, i * D + m0:i * D + m0 + mw],
                                   pb[bank + mh][:, 0:mw], ALU.add))
                    p.bank_done(ta_, bank + mh)
                    out_add[i] = ta_

    out_toks = []
    for b2 in range(NT // 2):
        deps = [out_add[b2 * 2 + i_] for i_ in range(2)]
        out_toks.append(p.dma("sp", "q_x" if b2 % 2 == 0 else "q_misc", deps,
                              lambda s, b2=b2: s.dma_start(
                                  out_d.ap()[b2 * 256:(b2 + 1) * 256, :]
                                  .rearrange("(a q) d -> q a d", q=128),
                                  x1[:, b2 * 2 * D:(b2 + 1) * 2 * D])))
    p._emit("sp", out_toks, lambda s: None, None, 0)

    with nc.Block() as blk:
        p.realize(blk)
    nc.finalize()
    return nc


def kernel(L0_attn_x, L0_attn_Wk, L0_attn_Wq, L0_attn_Wv, L0_ffn_gate_w,
           L0_ffn_w1, L0_ffn_w2, L0_ffn_w3, L0_c, L0_wmod):
    B = L0_attn_x.shape[0]
    nc = build_nc()
    in_maps = []
    for b in range(B):
        in_maps.append({
            "x": np.ascontiguousarray(np.asarray(L0_attn_x)[b]),
            "c": np.ascontiguousarray(np.asarray(L0_c)[b:b + 1]),
            "wmod": np.asarray(L0_wmod),
            "wk": np.asarray(L0_attn_Wk),
            "wq": np.asarray(L0_attn_Wq),
            "wv": np.asarray(L0_attn_Wv),
            "gw": np.asarray(L0_ffn_gate_w),
            "w1": np.asarray(L0_ffn_w1),
            "w2": np.asarray(L0_ffn_w2),
            "w3": np.asarray(L0_ffn_w3),
        })
    res = run_bass_kernel_spmd(nc, in_maps, core_ids=list(range(B)))
    return np.stack([np.asarray(r["out"]) for r in res.results], axis=0)


# version 20
# speedup vs baseline: 1.1766x; 1.0027x over previous
import numpy as np
import concourse.bass as bass
import concourse.mybir as mybir
from concourse.bass_utils import run_bass_kernel_spmd

F32 = mybir.dt.float32
BF16 = mybir.dt.bfloat16
F16 = mybir.dt.float16
AF = mybir.ActivationFunctionType
ALU = mybir.AluOpType
AX = mybir.AxisListType

NEG = -1.0e30


def splits(total, step):
    out = []
    o = 0
    while o < total:
        out.append((o, min(step, total - o)))
        o += step
    return out


class PB:

    ENGS = ("pe", "dv", "ac", "gp", "sp")

    def __init__(self, nc):
        self.nc = nc
        self.steps = {e: [] for e in self.ENGS}
        self.sem = {}
        self.cnt = {}
        for e in self.ENGS:
            self.sem[e] = nc.alloc_semaphore(f"s_{e}")
            self.cnt[e] = 0
        for q in ("q_x", "q_wv0", "q_wv1", "q_wk", "q_wg0a", "q_wg0b",
                  "q_wm0", "q_wm1", "q_wm2", "q_wm3",
                  "q_wg1a", "q_wg1b", "q_w20", "q_w21", "q_w22", "q_w23", "q_w24",
                  "q_misc", "q_gw", "q_pst"):
            self.sem[q] = nc.alloc_semaphore(f"s_{q}")
            self.cnt[q] = 0
        self.waited = {e: {} for e in self.ENGS}
        self.selfser = {"dv", "ac", "gp"}
        self.bguard = {}

    def _emit(self, eng, deps, fn, inc_key, inc_amt):
        deps = [d for d in deps if d is not None]
        if eng in self.selfser and self.cnt[eng] > 0:
            deps = deps + [(eng, self.cnt[eng])]
        if inc_key is not None:
            self.cnt[inc_key] += inc_amt
        val_after = self.cnt[inc_key] if inc_key is not None else None
        w = self.waited[eng]
        needed = {}
        for (k, v) in deps:
            if v > w.get(k, 0):
                needed[k] = max(needed.get(k, 0), v)
        for k, v in needed.items():
            w[k] = v
        self.steps[eng].append((sorted(needed.items()), fn, inc_key, inc_amt))
        return (inc_key, val_after) if inc_key is not None else None

    def op(self, eng, deps, fn):
        return self._emit(eng, deps, fn, eng, 1)

    def dma(self, eng, queue, deps, fn):
        return self._emit(eng, deps, fn, queue, 16)

    def bank_dep(self, *banks):
        return [self.bguard.get(b) for b in banks]

    def bank_done(self, tok, *banks):
        for b in banks:
            self.bguard[b] = tok

    def realize(self, block):
        sems = self.sem
        used_pe = set()
        for e in self.ENGS:
            for (needed, _f, _k, _a) in self.steps[e]:
                for (k, v) in needed:
                    if k == "pe":
                        used_pe.add(v)

        def body(engobj, eng):
            pe_val = 0
            pending = 0
            n = len(self.steps[eng])
            for idx, (needed, fn, inc_key, inc_amt) in enumerate(self.steps[eng]):
                for (k, v) in needed:
                    engobj.wait_ge(sems[k], v)
                ins = fn(engobj)
                if inc_key == "pe":
                    pe_val += inc_amt
                    pending += inc_amt
                    if pe_val in used_pe or idx == n - 1:
                        ins.then_inc(sems["pe"], pending)
                        pending = 0
                elif inc_key is not None:
                    ins.then_inc(sems[inc_key], inc_amt)

        block.tensor(lambda t: body(t, "pe"))
        block.vector(lambda v: body(v, "dv"))
        block.scalar(lambda s: body(s, "ac"))
        block.gpsimd(lambda g: body(g, "gp"))
        block.sync(lambda s: body(s, "sp"))


def build_nc(S=2048, D=1024, F=3584, E=8, CAP=640, dbg=False):
    NT = S // 128
    MC = D // 128
    FC = F // 128
    FG = F // 512
    CC = (CAP + 127) // 128
    CH = CAP // 2
    LNC = splits(D, 512)
    MH = splits(D, 512)
    CS = splits(CAP, 512)
    SH = 2 if NT % 2 == 0 else 1
    ISQ = float(1.0 / np.sqrt(np.float32(D)))
    assert CH <= 512 and F % 512 == 0 and S % 512 == 0 and D % 128 == 0

    nc = bass.Bass("TRN2", target_bir_lowering=False)

    x_in = nc.dram_tensor("x", [S, D], F32, kind="ExternalInput")
    c_in = nc.dram_tensor("c", [1, D], F32, kind="ExternalInput")
    wmod_in = nc.dram_tensor("wmod", [D, 6 * D], F32, kind="ExternalInput")
    wk_in = nc.dram_tensor("wk", [D, D], F32, kind="ExternalInput")
    wq_in = nc.dram_tensor("wq", [D, D], F32, kind="ExternalInput")
    wv_in = nc.dram_tensor("wv", [D, D], F32, kind="ExternalInput")
    gw_in = nc.dram_tensor("gw", [E, D], F32, kind="ExternalInput")
    w1_in = nc.dram_tensor("w1", [E, D, F], F32, kind="ExternalInput")
    w2_in = nc.dram_tensor("w2", [E, F, D], F32, kind="ExternalInput")
    w3_in = nc.dram_tensor("w3", [E, D, F], F32, kind="ExternalInput")
    out_d = nc.dram_tensor("out", [S, D], F32, kind="ExternalOutput")
    x1_d = nc.dram_tensor("x1spill", [S, D], F32, kind="Internal")
    dbg_o = nc.dram_tensor("dbg_o", [S, D], F32, kind="Internal") if dbg else None
    dbg_P = nc.dram_tensor("dbg_P", [128, S], F32, kind="Internal") if dbg else None
    dbg_V = nc.dram_tensor("dbg_V", [128, S * D // 128], mybir.dt.bfloat16, kind="Internal") if dbg else None
    dbg_q = nc.dram_tensor("dbg_q", [128, D], mybir.dt.bfloat16, kind="Internal") if dbg else None
    dbg_nm = nc.dram_tensor("dbg_nm", [128, 2], F32, kind="Internal") if dbg else None
    dbg_tok = [None]

    ARENA = 212832
    base = nc.bump_sbuf(ARENA)[0]
    alloc_at = nc.alloc_sbuf_tensor_at
    off = [base]

    def place(sz):
        o = off[0]
        off[0] += (sz + 31) & ~31
        return o

    r0 = place(max(NT * D * 4, 2 * MC * S * 2, 2 * 2048 * 4))
    x1 = alloc_at("x1", [128, NT * D], F32, offset=r0)
    hT = alloc_at("hT", [128, MC * S], BF16, offset=r0)
    KT = alloc_at("KT", [128, MC * S], BF16, offset=r0 + MC * S * 2)
    NSLOT = 4
    wmod_b = alloc_at("wmod_b", [128, NSLOT * 1024], BF16,
                      offset=r0 + NT * D * 4 - NSLOT * 1024 * 2)

    r1 = place(max(NT * D * 2, FC * CAP * 2,
                   CC * (S // SH) * 2 + S * 2 + S * 4, 4 * D * 4))
    V = alloc_at("V", [128, NT * D], BF16, offset=r1)
    gT = alloc_at("gT", [128, FC * CAP], BF16, offset=r1)
    CM = alloc_at("CM", [128, CC * (S // SH)], BF16, offset=r1)

    DHALF = min(512, D)
    NDH = D // DHALF
    atn2 = MC * D * 2 + 2 * MC * DHALF * 2
    r2 = place(max(atn2, 2 * 2 * MC * 512 * 2))
    Wqb = alloc_at("Wqb", [128, MC * D], BF16, offset=r2)
    Wkv = [alloc_at(f"Wkv{h}", [128, MC * DHALF], BF16,
                    offset=r2 + MC * D * 2 + h * MC * DHALF * 2)
           for h in range(2)]
    Pts_hi = [alloc_at(f"Pt{2 + k}", [128, S], BF16,
                       offset=r2 + MC * D * 2 + k * S * 2) for k in range(2)]
    assert 2 * MC * DHALF * 2 >= 2 * S * 2
    GB = MC * 512 * 2
    wgbuf = []
    for i in range(2):
        wgbuf.append((alloc_at(f"w1g{i}", [128, MC * 512], BF16,
                               offset=r2 + i * 2 * GB),
                      alloc_at(f"w3g{i}", [128, MC * 512], BF16,
                               offset=r2 + i * 2 * GB + GB)))

    atn3 = 2 * D * 4 + D * 4 + S * 4 + 4 * D * 2 + NT * 128 * 2
    r3 = place(max(NT * D * 2, atn3))
    h2b = alloc_at("h2b", [128, NT * D], BF16, offset=r3)
    xin = alloc_at("xin", [128, 2 * D], F32, offset=r3)
    o_sb = alloc_at("o_sb", [128, D], F32, offset=r3 + 2 * D * 4)
    Pts = [alloc_at(f"Pt{k}", [128, S], BF16,
                    offset=r3 + 3 * D * 4 + k * S * 2) for k in range(2)] + Pts_hi
    hs1 = alloc_at("hs1", [128, D], F32, offset=r3 + 3 * D * 4)
    qt4 = alloc_at("qt4", [128, 4 * D], BF16, offset=r3 + 3 * D * 4 + S * 4)
    mod_stage = alloc_at("mod_stage", [1, 512], F32, offset=r3 + 3 * D * 4 + S * 4)
    cl_scr = alloc_at("cl_scr", [128, D], F32,
                      offset=r3 + 3 * D * 4 + S * 4 + 2048)
    PT = alloc_at("PT", [128, NT * 128], BF16,
                  offset=r3 + 3 * D * 4 + S * 4 + 4 * D * 2)

    s1win = max(max(CAP, 512) * 2, CAP * 2)
    r4 = place(NT * CH * 2 + s1win)
    De = alloc_at("De", [128, NT * CH], BF16, offset=r4)
    w2s = alloc_at("w2s", [128, MC * 512], BF16, offset=r4)
    s1 = alloc_at("s1", [128, max(CAP, 512)], BF16, offset=r4 + NT * CH * 2)
    prob_row = alloc_at("prob_row", [1, CAP], BF16, offset=r4 + NT * CH * 2)
    assert FC * CAP * 2 >= CC * (S // SH) * 2 + S * 2
    mkT_f = alloc_at("mkT_f", [E, S], F32, offset=r1 + CC * (S // SH) * 2 + S * 2)
    gw_s = alloc_at("gw_s", [E, D], F32, offset=r4)
    WkF = alloc_at("WkF", [128, MC * D], BF16, offset=r4)
    h2f_scr = [alloc_at(f"h2f_scr{k}", [128, D], F32, offset=r4 + k * 2 * D * 4)
               for k in range(2)]
    h2T_scr = [alloc_at(f"h2T_scr{k}", [128, D], F32,
                        offset=r4 + k * 2 * D * 4 + D * 4) for k in range(2)]

    r5 = place(max(MC * CAP * 2, CC * D * 2))
    disp = alloc_at("disp", [128, MC * CAP], BF16, offset=r5)
    out_e = alloc_at("out_e", [128, CC * D], BF16, offset=r5)

    rM = place(6 * D * 2)
    Mb = alloc_at("Mb", [128, 6 * D], BF16, offset=rM)

    def mslot(i):
        return Mb[:, i * D:(i + 1) * D]

    ones_f = alloc_at("ones_f", [1, 128], F32, offset=place(128 * 4))
    iotaP = alloc_at("iotaP", [8, 128], F32, offset=place(128 * 4))
    sel = alloc_at("sel", [8, 128], F16, offset=place(128 * 2))
    ones_b = alloc_at("ones_b", [1, 128], BF16, offset=place(128 * 2))
    posT = alloc_at("posT", [E, S], F16, offset=place(S * 2))

    rR = place(4 * NT * E * 4)
    probs_a = alloc_at("probs_a", [128, NT * E], F32, offset=rR)
    mk_a = alloc_at("mk_a", [128, NT * E], F32, offset=rR + NT * E * 4)
    C_a = alloc_at("C_a", [128, NT * E], F32, offset=rR + 2 * NT * E * 4)
    pos_a = alloc_at("pos_a", [128, NT * E], F32, offset=rR + 3 * NT * E * 4)
    lg_a = C_a

    identf = alloc_at("identf", [128, 128], F32, offset=place(128 * 4))
    identb = alloc_at("identb", [128, 128], BF16, offset=place(128 * 2))
    iw = max(CH, 128)
    iotac = alloc_at("iotac", [128, iw], F32, offset=place(iw * 4))
    pcol = alloc_at("pcol", [128, 1], F32, offset=place(4))
    ccol = alloc_at("ccol", [128, CC], F32, offset=place(CC * 4))
    strictU = alloc_at("strictU", [128, 128], F32, offset=place(128 * 4))
    gwT = alloc_at("gwT", [128, MC * E], F32, offset=place(MC * E * 4))
    ct = alloc_at("ct", [128, MC], F32, offset=place(MC * 4))
    cts = alloc_at("cts", [128, MC], BF16, offset=place(MC * 2))
    scr = alloc_at("scr", [128, 56], F32, offset=place(56 * 4))
    probs_b = alloc_at("probs_b", [128, NT * E], BF16, offset=place(NT * E * 2))
    bn6 = alloc_at("bn6", [128, 12 * len(LNC)], F32, offset=place(12 * len(LNC) * 4))
    st2 = alloc_at("st2", [128, 4], F32, offset=place(4 * 4))
    assert off[0] - base <= ARENA, f"SBUF over budget: {off[0] - base}"

    pb = [nc.alloc_psum_tensor(f"pb{i}", [128, 512], F32) for i in range(8)]
    pbb = [t.bitcast(BF16) for t in pb]

    p = PB(nc)

    def g_iota(ap, pattern, base, chm):
        return lambda g: g.iota(ap, pattern=pattern, base=base,
                                channel_multiplier=chm,
                                allow_small_or_imprecise_dtypes=True)

    t_iotac = p.op("gp", [], g_iota(iotac[:], [[1, iw]], 0, 0))
    t_pcol = p.op("gp", [], g_iota(pcol[:], [[0, 1]], 0, 1))
    t_ccol = p.op("gp", [], g_iota(ccol[:], [[128, CC]], 0, 1))
    t_ones = p.op("gp", [], lambda g: g.memset(ones_f[:], 1.0))
    t_iotaP = p.op("gp", [], g_iota(iotaP[:], [[0, 128]], 0, 1))
    t_onesb = p.op("gp", [], lambda g: g.memset(ones_b[:], 1.0))
    t_eps = p.op("gp", [], lambda g: g.memset(scr[:, 0:1], 1e-5))
    t_ct = p.dma("gp", "q_misc", [], lambda g: g.dma_start(
        ct[:, :], c_in.ap().rearrange("o (j q) -> o q j", q=128)[0],
        allow_slow_non_contiguous=True))
    t_gw = p.dma("gp", "q_gw", [], lambda g: g.dma_start(gw_s[:, :], gw_in.ap()))

    t_idf = p.op("dv", [t_iotac, t_pcol], lambda v: v.tensor_scalar(
        identf[:], iotac[:, 0:128], pcol[:], None, ALU.is_equal))
    t_idb = p.op("dv", [t_idf], lambda v: v.tensor_copy(identb[:], identf[:]))
    t_su = p.op("dv", [t_iotac, t_pcol], lambda v: v.tensor_scalar(
        strictU[:], iotac[:, 0:128], pcol[:], None, ALU.is_gt))
    t_dm = t_su

    t_gwT = []
    for j in range(MC):
        tt = p.op("pe", [t_gw, t_idf] + p.bank_dep(0),
                  lambda t, j=j: t.transpose(
                      pb[0][:, 0:E], gw_s[:, j * 128:(j + 1) * 128],
                      identf[0:E, 0:E]))
        te = p.op("dv", [tt], lambda v, j=j: v.tensor_copy(
            gwT[:, j * E:(j + 1) * E], pb[0][:, 0:E]))
        p.bank_done(te, 0)
        t_gwT.append(te)

    def wdma_half(dst, sem, src, half, deps):
        return p.dma("gp", sem, deps, lambda g: g.dma_start(
            dst[:], src.ap()[:, half * DHALF:(half + 1) * DHALF]
            .rearrange("(j q) d -> q j d", q=128)))

    wv_tok = [wdma_half(Wkv[h], f"q_wv{h}", wv_in, h, []) for h in range(NDH)]

    PIECE_SLOT = {0: 1, 1: 0, 2: 2, 3: 4, 4: 3, 5: 5}
    t_cts = p.op("ac", [t_ct, t_gw], lambda s: s.activation(cts[:], ct[:], AF.Silu))
    SUB = min(512, D)
    WB = 2 * SUB
    NCC = (6 * D) // WB
    pieces_done = {pc: [] for pc in range(6)}
    t_clamp = []
    wm_pe = {}
    mst = {"bc": None}

    def emit_clamp(pc):
        r_ = mslot(pc)
        sg = cl_scr[:]
        a1 = p.op("dv", pieces_done[pc] + t_clamp,
                  lambda v, r_=r_, sg=sg: v.tensor_scalar(
                      sg, r_, 0.0, 2.0, ALU.is_ge, ALU.mult))
        a2 = p.op("dv", [a1], lambda v, sg=sg: v.tensor_scalar(
            sg, sg, 1.0, None, ALU.subtract))
        a3 = p.op("dv", [a2], lambda v, r_=r_, sg=sg: v.tensor_tensor(
            r_, r_, sg, ALU.mult))
        a4 = p.op("dv", [a3], lambda v, r_=r_: v.tensor_scalar(
            r_, r_, 0.25, None, ALU.max))
        a5 = p.op("dv", [a4], lambda v, r_=r_, sg=sg: v.tensor_tensor(
            r_, r_, sg, ALU.mult))
        t_clamp.append(a5)

    def emit_piece_bcast(cc3, mmlast):
        for n in range(2):
            colg = cc3 * WB + n * SUB
            piece, incol = colg // D, colg % D
            mslot_i = PIECE_SLOT[piece]
            plus1 = piece in (1, 4)
            ev = p.op("dv", [mmlast, mst["bc"]], lambda v, n=n: v.tensor_copy(
                mod_stage[0:1, 0:SUB], pb[4 + n][0:1, 0:SUB]))
            p.bank_done(ev, 4 + n)
            mm = p.op("pe", [ev, t_ones] + p.bank_dep(6),
                      lambda t: t.matmul(pb[6][:, 0:SUB], ones_f[:],
                                         mod_stage[0:1, 0:SUB],
                                         start=True, stop=True))
            mst["bc"] = mm
            if plus1:
                e2 = p.op("dv", [mm], lambda v, mslot_i=mslot_i, incol=incol:
                          v.tensor_scalar(
                              Mb[:, mslot_i * D + incol:mslot_i * D + incol + SUB],
                              pb[6][:, 0:SUB], 1.0, None, ALU.add))
            else:
                e2 = p.op("dv", [mm], lambda v, mslot_i=mslot_i, incol=incol:
                          v.tensor_copy(
                              Mb[:, mslot_i * D + incol:mslot_i * D + incol + SUB],
                              pb[6][:, 0:SUB]))
            p.bank_done(e2, 6)
            pieces_done[piece].append(e2)
        if cc3 in (2, 5):
            emit_clamp(cc3)

    def emit_mod_pair(cc3, j):
        idx = cc3 * MC + j
        slot = idx % NSLOT
        tw = p.dma("gp", f"q_wm{slot}", [wm_pe.get(idx - NSLOT)],
                   lambda g, cc3=cc3, j=j, slot=slot: g.dma_start(
                       wmod_b[:, slot * WB:slot * WB + WB],
                       wmod_in.ap()[j * 128:(j + 1) * 128,
                                    cc3 * WB:(cc3 + 1) * WB]))
        last = None
        for n in range(2):
            bdep = p.bank_dep(4 + n) if j == 0 else []
            last = p.op("pe", [tw, t_cts] + bdep,
                        lambda t, j=j, n=n, slot=slot: t.matmul(
                            pb[4 + n][0:1, 0:SUB], cts[:, j:j + 1],
                            wmod_b[:, slot * WB + n * SUB:
                                   slot * WB + (n + 1) * SUB],
                            start=(j == 0), stop=(j == MC - 1)))
        wm_pe[idx] = last
        if j == MC - 1:
            emit_piece_bcast(cc3, last)

    MOD_ILV = NCC
    for cc3 in range(MOD_ILV):
        for j in range(MC):
            emit_mod_pair(cc3, j)
    mod_rest = [(cc3, j) for cc3 in range(MOD_ILV, NCC) for j in range(MC)]

    wq_toks = [p.dma("gp", "q_wk", [], lambda g: g.dma_start(
        Wqb[:].rearrange("q (j d) -> q j d", j=MC),
        wq_in.ap().rearrange("(j q) d -> q j d", q=128)))]
    t_wkF = p.dma("gp", "q_wg0b", t_gwT, lambda g: g.dma_start(
        WkF[:].rearrange("q (j d) -> q j d", j=MC),
        wk_in.ap().rearrange("(j q) d -> q j d", q=128)))

    modA = pieces_done[0] + pieces_done[1]

    def ln_stats(xt_ap, dep, war_dep, par=0):
        o = 8 * par
        nl = len(LNC)
        tb = []
        for k, (c0, cw) in enumerate(LNC):
            tb.append(p.op("dv", [dep] + ([war_dep] if k == 0 else []),
                           lambda v, k=k, c0=c0, cw=cw: v.bn_stats(
                               bn6[:, (par * nl + k) * 6:(par * nl + k + 1) * 6],
                               xt_ap[:, c0:c0 + cw])))
        ta = p.op("dv", tb, lambda v: v.bn_aggr(
            st2[:, 2 * par:2 * par + 2],
            bn6[:, par * nl * 6:(par + 1) * nl * 6]))
        s1_ = p.op("ac", [ta, t_eps], lambda s: s.activation(
            scr[:, 1 + o:2 + o], st2[:, 2 * par + 1:2 * par + 2], AF.Sqrt,
            bias=scr[:, 0:1]))
        s2_ = p.op("dv", [s1_], lambda v: v.reciprocal(
            scr[:, 2 + o:3 + o], scr[:, 1 + o:2 + o]))
        s3_ = p.op("dv", [s2_], lambda v: v.tensor_scalar(
            scr[:, 3 + o:4 + o], st2[:, 2 * par:2 * par + 1],
            scr[:, 2 + o:3 + o], -1.0, ALU.mult, ALU.mult))
        return s3_, s1_

    xin_tok = [None] * NT
    xin_free = {}
    v_pe_last = None
    v_last = None
    hT_last = [None] * NT
    prev_ts1 = None
    hscr = [o_sb[:], hs1[:]]
    th_tok = [None] * NT
    pend_evac = None

    def emit_a1_evacs(i):
        par = i % 2
        hs = hscr[par]
        last = None
        for jg in range(0, MC, 4):
            ng = min(4, MC - jg)
            bank = 2 * par + (jg // 4) % 2
            tt = None
            for j_i in range(ng):
                tt = p.op("pe", [th_tok[i], t_idf] +
                          (p.bank_dep(bank) if j_i == 0 else []),
                          lambda t, j=jg + j_i, j_i=j_i, bank=bank, hs=hs:
                          t.transpose(
                              pb[bank][:, j_i * 128:(j_i + 1) * 128],
                              hs[:, j * 128:(j + 1) * 128], identf[:]))
            ev = p.op("ac", [tt], lambda v, jg=jg, ng=ng, i=i, bank=bank:
                      v.copy(
                          hT[:].rearrange("q (j s) -> q j s", j=MC)
                          [:, jg:jg + ng, i * 128:(i + 1) * 128],
                          pb[bank][:, 0:ng * 128]
                          .rearrange("q (j s) -> q j s", j=ng)))
            p.bank_done(ev, bank)
            hT_tr_done[par] = tt
            last = ev
        hT_last[i] = last
        for half in range(NDH):
            bank = 2 * par + half
            vmm = None
            for j in range(MC):
                vmm = p.op("pe", [wv_tok[half], last] +
                           (p.bank_dep(bank) if j == 0 else []),
                           lambda t, j=j, i=i, half=half, bank=bank: t.matmul(
                               pb[bank][:, 0:DHALF],
                               hT[:, j * S + i * 128:j * S + (i + 1) * 128],
                               Wkv[half][:, j * DHALF:(j + 1) * DHALF],
                               start=(j == 0), stop=(j == MC - 1)))
            ev = p.op("ac", [vmm], lambda v, i=i, half=half, bank=bank:
                      v.copy(
                          V[:, i * D + half * DHALF:i * D + (half + 1) * DHALF],
                          pb[bank][:, 0:DHALF]))
            p.bank_done(ev, bank)
            return_vals[0] = vmm
            return_vals[1] = ev

    return_vals = [None, None]
    hT_tr_done = [None, None]
    for i in range(NT):
        slot = i % 2
        par = i % 2
        hs = hscr[par]
        o = 8 * par
        dep = [xin_free.get(slot), xin_tok[i - 1] if i else None]
        td = p.dma("sp", "q_x", dep, lambda s, i=i: s.dma_start(
            xin[:, (i % 2) * D:((i % 2) + 1) * D],
            x_in.ap()[i * 128:(i + 1) * 128, :]))
        xin_tok[i] = td
        xt = xin[:, slot * D:(slot + 1) * D]
        s3_, prev_ts1 = ln_stats(xt, xin_tok[i], prev_ts1, par)
        tl = p.op("dv", [s3_], lambda v, xt=xt, hs=hs, o=o: v.tensor_scalar(
            hs, xt, scr[:, 2 + o:3 + o], scr[:, 3 + o:4 + o],
            ALU.mult, ALU.add))
        tm = p.op("dv", [tl] + modA, lambda v, hs=hs: v.tensor_tensor(
            hs, hs, mslot(0), ALU.mult))
        th = p.op("dv", [tm, hT_tr_done[par]], lambda v, hs=hs: v.tensor_tensor(
            hs, hs, mslot(1), ALU.add))
        th_tok[i] = th
        xin_free[slot] = th
        if pend_evac is not None:
            emit_a1_evacs(pend_evac)
        pend_evac = i
        for _ in range(2):
            if mod_rest:
                emit_mod_pair(*mod_rest.pop(0))
    emit_a1_evacs(pend_evac)
    while mod_rest:
        emit_mod_pair(*mod_rest.pop(0))
    mod_bc_final = mst["bc"]
    bc_ga = [t_clamp[0]]
    bc_gb = [t_clamp[1]]
    modB = pieces_done[3] + pieces_done[4]
    v_pe_last, v_last = return_vals
    hT_done = [hT_last[NT - 1]]

    kt_last = None
    kt_pe_last = None
    for half in range(NDH):
        for jd in range(DHALF // 128):
            d_chunk = half * (DHALF // 128) + jd
            for sb in range(S // 512):
                bank = sb % 2
                last = None
                for j in range(MC):
                    last = p.op("pe", [t_wkF] + hT_done +
                                (p.bank_dep(bank) if j == 0 else []),
                                lambda t, j=j, jd=jd, sb=sb, bank=bank, half=half:
                                t.matmul(
                                    pb[bank][:, :],
                                    WkF[:, j * D + half * DHALF + jd * 128:
                                        j * D + half * DHALF + (jd + 1) * 128],
                                    hT[:, j * S + sb * 512:j * S + (sb + 1) * 512],
                                    start=(j == 0), stop=(j == MC - 1)))
                kt_pe_last = last
                ev = p.op("dv", [last] +
                          ([wm_pe[NCC * MC - 1]] if half == 1 else []),
                          lambda v, d_chunk=d_chunk, sb=sb, bank=bank:
                          v.tensor_copy(
                              KT[:, d_chunk * S + sb * 512:d_chunk * S + (sb + 1) * 512],
                              pb[bank][:, :]))
                p.bank_done(ev, bank)
                kt_last = ev

    if dbg:
        p.dma("sp", "q_pst", [v_last], lambda s: s.dma_start(dbg_V.ap(), V[:]))

    xi2_tok = [None] * NT
    xi2_free = {}
    spill_tok = {}
    pt_tr_last = [None] * 4
    exp_last_par = [None] * 4
    AC = {0: dict(cm=8, acc=12, rm=4, nm=5, rs=6, ri=7),
          1: dict(cm=16, acc=20, rm=24, nm=25, rs=26, ri=27),
          2: dict(cm=28, acc=32, rm=36, nm=37, rs=38, ri=39),
          3: dict(cm=40, acc=44, rm=48, nm=49, rs=50, ri=51)}
    pend = None
    qt_pe_final = [None]
    scores_final = [None]
    x1_tok = [None] * NT

    def emit_ptav(ctx):
        qi, exp_toks, nb = ctx
        par = qi % 4
        C = AC[par]
        slot = qi % 2
        Ptb = Pts[par]
        palt = (2 * qi) % 6
        ngr = (qi + 1 + 3) // 4
        evs = []
        tr_last = None
        for g4 in range(ngr):
            bank = 6 if g4 % 2 == 0 else palt
            kts = list(range(g4 * 4, min(g4 * 4 + 4, qi + 1)))
            for k_i, kt in enumerate(kts):
                tr_last = p.op("pe",
                               [exp_toks[min(kt * 128 // 512, nb - 1)], t_idb] +
                               (p.bank_dep(bank) if k_i == 0 else []),
                               lambda t, kt=kt, k_i=k_i, bank=bank: t.transpose(
                                   pbb[bank][:, k_i * 128:(k_i + 1) * 128],
                                   Ptb[:, kt * 128:(kt + 1) * 128], identb[:]))
            nk = len(kts)
            ev = p.op("dv", [tr_last], lambda v, g4=g4, nk=nk, bank=bank:
                      v.tensor_copy(PT[:, g4 * 512:g4 * 512 + nk * 128],
                                    pbb[bank][:, 0:nk * 128]))
            p.bank_done(ev, bank)
            evs.append(ev)
        pt_tr_last[par] = tr_last
        tosb = []
        for mh, (m0, mw) in enumerate(MH):
            bank_av = 7 if mh == 0 else 6
            av_last = None
            for kt in range(qi + 1):
                dep = [evs[kt // 4]] if kt % 4 == 0 else []
                if kt == 0:
                    dep = dep + [v_last] + p.bank_dep(bank_av)
                av_last = p.op("pe", dep,
                               lambda t, kt=kt, qi=qi, m0=m0, mw=mw,
                               bank_av=bank_av: t.matmul(
                                   pb[bank_av][:, 0:mw],
                                   PT[:, kt * 128:(kt + 1) * 128],
                                   V[:, kt * D + m0:kt * D + m0 + mw],
                                   start=(kt == 0), stop=(kt == qi)))
            ts_ = p.op("dv", [av_last], lambda v, m0=m0, mw=mw, C=C,
                       bank_av=bank_av: v.tensor_scalar(
                           o_sb[:, m0:m0 + mw], pb[bank_av][:, 0:mw],
                           scr[:, C["ri"]:C["ri"] + 1], None, ALU.mult))
            p.bank_done(ts_, bank_av)
            tosb.append(ts_)
        tga = p.op("dv", tosb + bc_ga, lambda v: v.tensor_tensor(
            o_sb[:], o_sb[:], mslot(2), ALU.mult))
        if dbg:
            dbg_tok[0] = p.dma("sp", "q_pst", [tga, dbg_tok[0]],
                               lambda s, qi=qi: s.dma_start(
                                   dbg_o.ap()[qi * 128:(qi + 1) * 128, :],
                                   o_sb[:]))
        txa = p.op("dv", [tga, xi2_tok[qi], spill_tok.get(qi - 1)] +
                   (dbg_tok if dbg else []),
                   lambda v, slot=slot:
                   v.tensor_tensor(xin[:, slot * D:(slot + 1) * D], o_sb[:],
                                   xin[:, slot * D:(slot + 1) * D], ALU.add))
        xi2_free[slot] = txa
        spill_tok[qi] = p.dma("sp", "q_x", [txa], lambda s, qi=qi, slot=slot:
                              s.dma_start(
                                  x1_d.ap()[qi * 128:(qi + 1) * 128, :],
                                  xin[:, slot * D:(slot + 1) * D]))

    for qi in range(NT):
        slot = qi % 2
        par = qi % 4
        C = AC[par]
        dep = [xi2_free.get(slot) if qi >= 2 else xin_free.get(slot),
               spill_tok.get(qi - 2),
               xi2_tok[qi - 1] if qi else xin_tok[NT - 1]]
        td = p.dma("sp", "q_x", dep, lambda s, qi=qi, slot=slot: s.dma_start(
            xin[:, slot * D:(slot + 1) * D],
            x_in.ap()[qi * 128:(qi + 1) * 128, :]))
        xi2_tok[qi] = td
        keys = (qi + 1) * 128
        kblocks = splits(keys, 512)
        nb = len(kblocks)
        pool = [(2 * qi + kb) % 6 for kb in range(nb)]
        if qi % 4 == 0:
            b4 = qi // 4
            qt_ev = []
            for jd in range(MC):
                bank = pool[jd % min(2, nb)]
                last = None
                for j in range(MC):
                    last = p.op("pe", wq_toks + hT_done +
                                (p.bank_dep(bank) if j == 0 else []),
                                lambda t, j=j, jd=jd, b4=b4, bank=bank: t.matmul(
                                    pb[bank][:, :],
                                    Wqb[:, j * D + jd * 128:j * D + (jd + 1) * 128],
                                    hT[:, j * S + b4 * 512:j * S + (b4 + 1) * 512],
                                    start=(j == 0), stop=(j == MC - 1)))
                ev = p.op("dv", [last] +
                          ([mod_bc_final] if b4 == 0 and jd == 0 else []),
                          lambda v, jd=jd, bank=bank: v.tensor_copy(
                              qt4[:, jd * 512:(jd + 1) * 512], pb[bank][:, :]))
                p.bank_done(ev, bank)
                qt_ev.append(ev)
            qt_pe_final[0] = last
        blk_pe = [None] * nb
        for kb in list(range(2, nb)) + list(range(min(2, nb))):
            k0, kw = kblocks[kb]
            bank = pool[kb]
            last = None
            for jd in range(MC):
                last = p.op("pe", qt_ev + [kt_last] +
                            (p.bank_dep(bank) if jd == 0 else []),
                            lambda t, jd=jd, k0=k0, kw=kw, bank=bank, qi=qi:
                            t.matmul(
                                pb[bank][:, 0:kw],
                                qt4[:, jd * 512 + (qi % 4) * 128:
                                    jd * 512 + (qi % 4) * 128 + 128],
                                KT[:, jd * S + k0:jd * S + k0 + kw],
                                start=(jd == 0), stop=(jd == MC - 1)))
            blk_pe[kb] = last
        if qi == NT - 1:
            scores_final[0] = blk_pe[-1]
        lw = kblocks[-1][1]
        lbank = pool[nb - 1]
        t_mask = p.op("dv", [blk_pe[-1], t_dm], lambda v, lbank=lbank, lw=lw:
                      v.scalar_tensor_tensor(pb[lbank][:, lw - 128:lw],
                                             strictU[:], float(NEG),
                                             pb[lbank][:, lw - 128:lw],
                                             ALU.mult, ALU.add))
        cm_toks = []
        for kb, (k0, kw) in enumerate(kblocks):
            dep = [t_mask] if kb == nb - 1 else [blk_pe[kb]]
            cm_toks.append(p.op("dv", dep, lambda v, kb=kb, kw=kw, C=C,
                                bank=pool[kb]: v.reduce_max(
                                    scr[:, C["cm"] + kb:C["cm"] + kb + 1],
                                    pb[bank][:, 0:kw], axis=AX.X)))
        t_rm = p.op("dv", cm_toks, lambda v, nb=nb, C=C: v.reduce_max(
            scr[:, C["rm"]:C["rm"] + 1], scr[:, C["cm"]:C["cm"] + nb], axis=AX.X))
        t_nm = p.op("dv", [t_rm, exp_last_par[par]], lambda v, C=C: v.tensor_scalar(
            scr[:, C["nm"]:C["nm"] + 1], scr[:, C["rm"]:C["rm"] + 1],
            -ISQ, None, ALU.mult))
        exp_toks = []
        for kb, (k0, kw) in enumerate(kblocks):
            ex = p.op("ac", [t_nm, pt_tr_last[par]],
                      lambda s, kb=kb, k0=k0, kw=kw, C=C, par=par,
                      bank=pool[kb]: s.activation(
                          Pts[par][:, k0:k0 + kw], pb[bank][:, 0:kw], AF.Exp,
                          bias=scr[:, C["nm"]:C["nm"] + 1], scale=ISQ,
                          accum_out=scr[:, C["acc"] + kb:C["acc"] + kb + 1]))
            p.bank_done(ex, pool[kb])
            exp_toks.append(ex)
        exp_last_par[par] = exp_toks[-1]
        if dbg and qi == 1:
            p.dma("sp", "q_pst", exp_toks, lambda s, keys=keys: s.dma_start(
                dbg_P.ap()[:, 0:keys], Pts[1][:, 0:keys]))
            p.dma("sp", "q_pst", exp_toks + qt_ev + [t_nm],
                  lambda s, qi=qi: s.dma_start(
                      dbg_q.ap(),
                      qt4[:].rearrange("q (jd four d) -> q jd four d",
                                       four=4, d=128)[:, :, qi % 4]))
            p.dma("sp", "q_pst", exp_toks + [t_nm],
                  lambda s: s.dma_start(dbg_nm.ap(), scr[:, 24:26]))
        t_rs = p.op("dv", exp_toks, lambda v, nb=nb, C=C: v.tensor_reduce(
            scr[:, C["rs"]:C["rs"] + 1], scr[:, C["acc"]:C["acc"] + nb],
            axis=AX.X, op=ALU.add))
        p.op("dv", [t_rs], lambda v, C=C: v.reciprocal(
            scr[:, C["ri"]:C["ri"] + 1], scr[:, C["rs"]:C["rs"] + 1]))
        if pend is not None:
            emit_ptav(pend)
        pend = (qi, exp_toks, nb)
        if qi == 13:
            for b4 in range(2):
                td_r = p.dma("sp", "q_x",
                             [qt_pe_final[0]] +
                             [spill_tok[b4 * 4 + k] for k in range(4)],
                             lambda s, b4=b4: s.dma_start(
                                 x1[:, b4 * 4 * D:(b4 + 1) * 4 * D],
                                 x1_d.ap()[b4 * 512:(b4 + 1) * 512, :]
                                 .rearrange("(a q) d -> q a d", q=128)))
                for i_ in range(4):
                    x1_tok[b4 * 4 + i_] = td_r
    emit_ptav(pend)
    attn_done = spill_tok[NT - 1]

    out_add = {i: None for i in range(NT)}
    gctr = [0]
    wg_tok = [None, None]
    wg_free = [None, None]

    def load_group(e, g, extra=()):
        buf = gctr[0] % 2
        gctr[0] += 1
        deps = [wg_free[buf]] + list(extra)
        d1 = p.dma("gp", f"q_wg{buf}a", deps, lambda gp, e=e, g=g, buf=buf:
                   gp.dma_start(wgbuf[buf][0][:],
                                w1_in.ap()[e, :, g * 512:(g + 1) * 512]
                                .rearrange("(j q) f -> q j f", q=128)))
        d3 = p.dma("gp", f"q_wg{buf}b", deps, lambda gp, e=e, g=g, buf=buf:
                   gp.dma_start(wgbuf[buf][1][:],
                                w3_in.ap()[e, :, g * 512:(g + 1) * 512]
                                .rearrange("(j q) f -> q j f", q=128)))
        wg_tok[buf] = (d1, d3)
        return buf

    load_group(0, 0, extra=[qt_pe_final[0]])
    if FG > 1:
        load_group(0, 1, extra=[v_pe_last, pt_tr_last[2], pt_tr_last[3]])

    for b4 in range(2, NT // 4):
        td_r = p.dma("sp", "q_x",
                     [scores_final[0]] +
                     [spill_tok[b4 * 4 + k] for k in range(4)],
                     lambda s, b4=b4: s.dma_start(
                         x1[:, b4 * 4 * D:(b4 + 1) * 4 * D],
                         x1_d.ap()[b4 * 512:(b4 + 1) * 512, :]
                         .rearrange("(a q) d -> q a d", q=128)))
        for i_ in range(4):
            x1_tok[b4 * 4 + i_] = td_r

    lg_tok = []
    rt_toks = []
    h2_done = None
    prev_ts1 = None
    router_last = {}
    tcast_tok = [None] * NT
    pend_lg = None

    def emit_routing4(g4, evs_lg):
        j0 = 4 * g4
        tiles = list(range(j0, j0 + 4))
        sl4 = lg_a[:, j0 * E:(j0 + 4) * E]
        t1 = p.op("dv", evs_lg, lambda v, j0=j0, sl4=sl4: v.reduce_max(
            scr[:, 16 + j0:20 + j0],
            sl4.rearrange("q (t e) -> q t e", e=E), axis=AX.X))
        p1s = []
        for j in tiles:
            p1s.append(p.op("dv", [t1], lambda v, j=j: v.tensor_scalar(
                probs_a[:, j * E:(j + 1) * E], lg_a[:, j * E:(j + 1) * E],
                scr[:, 16 + j:17 + j], None, ALU.subtract)))
        e1 = p.op("ac", p1s, lambda s, j0=j0: s.activation(
            probs_a[:, j0 * E:(j0 + 4) * E], probs_a[:, j0 * E:(j0 + 4) * E],
            AF.Exp))
        t2 = p.op("dv", [e1], lambda v, j0=j0: v.tensor_reduce(
            scr[:, 32 + j0:36 + j0],
            probs_a[:, j0 * E:(j0 + 4) * E].rearrange("q (t e) -> q t e", e=E),
            axis=AX.X, op=ALU.add))
        r_ = p.op("dv", [t2], lambda v, j0=j0: v.reciprocal(
            scr[:, 32 + j0:36 + j0], scr[:, 32 + j0:36 + j0]))
        last_n = None
        for j in tiles:
            last_n = p.op("dv", [r_], lambda v, j=j: v.tensor_scalar(
                probs_a[:, j * E:(j + 1) * E], probs_a[:, j * E:(j + 1) * E],
                scr[:, 32 + j:33 + j], None, ALU.mult))
        pbc = p.op("dv", [last_n], lambda v, j0=j0: v.tensor_copy(
            probs_b[:, j0 * E:(j0 + 4) * E], probs_a[:, j0 * E:(j0 + 4) * E]))
        ev = None
        for j in tiles:
            a_ = p.op("dv", [t1], lambda v, j=j: v.tensor_scalar(
                mk_a[:, j * E:(j + 1) * E], lg_a[:, j * E:(j + 1) * E],
                scr[:, 16 + j:17 + j], None, ALU.is_ge))
            b_ = p.op("dv", [a_], lambda v, j=j: v.scalar_tensor_tensor(
                pos_a[:, j * E:(j + 1) * E], mk_a[:, j * E:(j + 1) * E],
                float(NEG), lg_a[:, j * E:(j + 1) * E], ALU.mult, ALU.add))
            c_ = p.op("dv", [b_], lambda v, j=j: v.reduce_max(
                scr[:, 16 + j:17 + j], pos_a[:, j * E:(j + 1) * E], axis=AX.X))
            d_ = p.op("dv", [c_], lambda v, j=j: v.tensor_scalar(
                pos_a[:, j * E:(j + 1) * E], pos_a[:, j * E:(j + 1) * E],
                scr[:, 16 + j:17 + j], None, ALU.is_ge))
            m_ = p.op("dv", [d_], lambda v, j=j: v.tensor_tensor(
                mk_a[:, j * E:(j + 1) * E], mk_a[:, j * E:(j + 1) * E],
                pos_a[:, j * E:(j + 1) * E], ALU.add))
            tt = p.op("pe", [m_, t_idf] + p.bank_dep(6),
                      lambda t, j=j: t.transpose(
                          pb[6][0:E, 0:128], mk_a[:, j * E:(j + 1) * E],
                          identf[:]))
            ev = p.op("dv", [tt], lambda v, j=j: v.tensor_copy(
                mkT_f[:, j * 128:(j + 1) * 128], pb[6][0:E, 0:128]))
            p.bank_done(ev, 6)
        return pbc, ev
    for i in range(NT):
        par = i % 2
        h2f = h2f_scr[par]
        h2T = h2T_scr[par]
        xt = x1[:, i * D:(i + 1) * D]
        s3_, prev_ts1 = ln_stats(xt, x1_tok[i], prev_ts1, par)
        tl = p.op("dv", [s3_, kt_pe_last, router_last.get(i - 2),
                         tcast_tok[i - 2] if i >= 2 else None],
                  lambda v, xt=xt, h2f=h2f, par=par:
                  v.tensor_scalar(h2f[:], xt, scr[:, 2 + 8 * par:3 + 8 * par],
                                  scr[:, 3 + 8 * par:4 + 8 * par],
                                  ALU.mult, ALU.add))
        tm = p.op("gp", [tl] + modB, lambda g, h2f=h2f: g.tensor_tensor(
            h2f[:], h2f[:], mslot(3), ALU.mult))
        th = p.op("dv", [tm], lambda v, h2f=h2f: v.tensor_tensor(
            h2f[:], h2f[:], mslot(4), ALU.add))
        tcast = p.op("ac", [th, attn_done], lambda v, i=i, h2f=h2f: v.copy(
            h2b[:, i * D:(i + 1) * D], h2f[:]))
        tcast_tok[i] = tcast
        h2_done = tcast
        trs = []
        for jg in range(0, MC, 4):
            ng = min(4, MC - jg)
            bank = 2 * par + (jg // 4) % 2
            tt = None
            for j_i in range(ng):
                tt = p.op("pe", [th, t_idf] +
                          (p.bank_dep(bank) if j_i == 0 else []),
                          lambda t, j=jg + j_i, j_i=j_i, bank=bank, h2f=h2f:
                          t.transpose(
                              pb[bank][:, j_i * 128:(j_i + 1) * 128],
                              h2f[:, j * 128:(j + 1) * 128], identf[:]))
            ev = p.op("ac", [tt], lambda s, jg=jg, ng=ng, bank=bank, h2T=h2T:
                      s.copy(h2T[:, jg * 128:(jg + ng) * 128],
                             pb[bank][:, 0:ng * 128]))
            p.bank_done(ev, bank)
            trs.append(ev)
        last = None
        for j in range(MC):
            last = p.op("pe", [trs[j // 4]] + (t_gwT if i == 0 else []) +
                        (p.bank_dep(4 + par) if j == 0 else []),
                        lambda t, j=j, par=par, h2T=h2T: t.matmul(
                            pb[4 + par][:, 0:E],
                            h2T[:, j * 128:(j + 1) * 128],
                            gwT[:, j * E:(j + 1) * E],
                            start=(j == 0), stop=(j == MC - 1)))
        router_last[i] = last
        if pend_lg is not None:
            j, jpar = pend_lg
            ev = p.op("dv", [router_last[j]], lambda v, j=j, jpar=jpar:
                      v.tensor_copy(lg_a[:, j * E:(j + 1) * E],
                                    pb[4 + jpar][:, 0:E]))
            p.bank_done(ev, 4 + jpar)
            lg_tok.append(ev)
            if len(lg_tok) % 4 == 0:
                rt_toks.append(emit_routing4(len(lg_tok) // 4 - 1,
                                             lg_tok[-4:]))
        pend_lg = (i, par)
    j, jpar = pend_lg
    ev = p.op("dv", [router_last[j]], lambda v, j=j, jpar=jpar: v.tensor_copy(
        lg_a[:, j * E:(j + 1) * E], pb[4 + jpar][:, 0:E]))
    p.bank_done(ev, 4 + jpar)
    lg_tok.append(ev)
    rt_toks.append(emit_routing4(len(lg_tok) // 4 - 1, lg_tok[-4:]))
    norm_last = rt_toks[-1][0]

    t_scan = p.op("dv", [rt_toks[-1][1]], lambda v: v.tensor_tensor_scan(
        mkT_f[:], mkT_f[:], mkT_f[:], 0.0, ALU.add, ALU.bypass))
    C_last = None
    for i in range(NT):
        tt = p.op("pe", [t_scan, t_idf] + p.bank_dep(3),
                  lambda t, i=i: t.transpose(
                      pb[3][:, 0:E], mkT_f[0:E, i * 128:(i + 1) * 128],
                      identf[0:E, 0:E]))
        C_last = p.op("dv", [tt], lambda v, i=i: v.tensor_copy(
            C_a[:, i * E:(i + 1) * E], pb[3][:, 0:E]))
        p.bank_done(C_last, 3)
    C_last = p.op("dv", [C_last], lambda v: v.tensor_tensor(
        C_a[:], C_a[:], mk_a[:], ALU.subtract))
    tv1 = p.op("dv", [C_last], lambda v: v.tensor_scalar(
        pos_a[:], C_a[:], float(CAP), None, ALU.is_lt))
    tv2 = p.op("dv", [tv1], lambda v: v.tensor_tensor(
        pos_a[:], pos_a[:], mk_a[:], ALU.mult))
    tv3 = p.op("dv", [tv2], lambda v: v.tensor_scalar(
        C_a[:], C_a[:], 1.0, None, ALU.add))
    tv4 = p.op("dv", [tv3], lambda v: v.tensor_tensor(
        pos_a[:], pos_a[:], C_a[:], ALU.mult))
    tpos = p.op("dv", [tv4], lambda v: v.tensor_scalar(
        pos_a[:], pos_a[:], 1.0, None, ALU.subtract))
    posT_last = None
    for i in range(NT):
        tt = p.op("pe", [tpos, t_idf] + p.bank_dep(3),
                  lambda t, i=i: t.transpose(
                      pb[3][0:E, 0:128], pos_a[:, i * E:(i + 1) * E], identf[:]))
        posT_last = p.op("dv", [tt], lambda v, i=i: v.tensor_copy(
            posT[:, i * 128:(i + 1) * 128], pb[3][0:E, 0:128]))
        p.bank_done(posT_last, 3)

    w2_free = [None, None]

    def load_w2(e, g, half, lastbuf, deps):
        w1g, w3g = wgbuf[lastbuf]
        dst = w1g if half == 0 else w3g
        tok = p.dma("gp", f"q_w2{half}", [w2_free[half]] + deps,
                    lambda gp, e=e, g=g, dst=dst: gp.dma_start(
                        dst[:].rearrange("q (fj m) -> q fj m", fj=4),
                        w2_in.ap()[e, g * 512:(g + 1) * 512, :]
                        .rearrange("(fj q) m -> q fj m", q=128)))
        return tok, dst

    de_free = None
    combine_pe_last = None
    sel_free = [None]
    w2s_war = [None]

    first_expert_deps = [attn_done, h2_done, tpos, norm_last]

    for e in range(E):
        disp_ev_last = None
        ps_row_ev = []
        de_pe_last = None
        for chalf in range(2):
            de_toks = []
            for i in range(NT):
                dep = [tpos, t_iotac]
                if i == 0:
                    dep += [de_pe_last if chalf == 1 else de_free]
                    if chalf == 0:
                        dep += [w2s_war[0]]
                de_toks.append(p.op("dv", dep,
                                    lambda v, i=i, e=e, chalf=chalf:
                                    v.tensor_scalar(
                                        De[:, i * CH:(i + 1) * CH],
                                        iotac[:, 0:CH],
                                        pos_a[:, i * E + e:i * E + e + 1],
                                        float(-chalf * CH),
                                        ALU.subtract, ALU.is_equal)))
            for j in range(MC):
                bank = j % 2
                last = None
                for i in range(NT):
                    dep = [de_toks[i]]
                    if i == 0:
                        dep += p.bank_dep(bank)
                        if j == 0 and chalf == 0:
                            dep += first_expert_deps if e == 0 else [combine_pe_last]
                    last = p.op("pe", dep, lambda t, i=i, j=j, bank=bank: t.matmul(
                        pb[bank][:, 0:CH],
                        h2b[:, i * D + j * 128:i * D + (j + 1) * 128],
                        De[:, i * CH:(i + 1) * CH],
                        start=(i == 0), stop=(i == NT - 1)))
                ev = p.op("dv", [last], lambda v, j=j, chalf=chalf, bank=bank:
                          v.tensor_copy(
                              disp[:, j * CAP + chalf * CH:j * CAP + (chalf + 1) * CH],
                              pb[bank][:, 0:CH]))
                p.bank_done(ev, bank)
                disp_ev_last = ev
            last = None
            for i in range(NT):
                dep = [de_toks[i]]
                if i == 0:
                    dep += p.bank_dep(2)
                last = p.op("pe", dep, lambda t, i=i, e=e: t.matmul(
                    pb[2][0:1, 0:CH], probs_b[:, i * E + e:i * E + e + 1],
                    De[:, i * CH:(i + 1) * CH],
                    start=(i == 0), stop=(i == NT - 1)))
            de_pe_last = last
            ev = p.op("dv", [last], lambda v, chalf=chalf: v.tensor_copy(
                prob_row[0:1, chalf * CH:(chalf + 1) * CH], pb[2][0:1, 0:CH]))
            p.bank_done(ev, 2)
            ps_row_ev.append(ev)
        de_free = de_pe_last
        t_sel = p.op("dv", [t_iotaP, posT_last, sel_free[0]],
                     lambda v, e=e: v.tensor_scalar(
                         sel[:], iotaP[:], float(e), None, ALU.is_equal))

        def emit_cm_half(sh):
            shs = S // SH
            out = {}
            for sc, (sc0, scw) in enumerate(splits(shs, 512)):
                scg = sh * shs + sc0
                bank = 6 + sc
                mm = p.op("pe", [t_sel] + p.bank_dep(bank),
                          lambda t, scg=scg, scw=scw, bank=bank: t.matmul(
                              pb[bank][:, 0:scw], sel[:],
                              posT[0:E, scg:scg + scw],
                              start=True, stop=True))
                sel_free[0] = mm
                cm_last = None
                for cchunk in range(CC):
                    wdt = min(128, CAP - cchunk * 128)
                    cm_last = p.op("dv", [mm, t_ccol],
                                   lambda v, cchunk=cchunk, sc0=sc0, scw=scw,
                                   wdt=wdt, bank=bank: v.tensor_scalar(
                                       CM[0:wdt, cchunk * shs + sc0:
                                          cchunk * shs + sc0 + scw],
                                       pb[bank][0:wdt, 0:scw],
                                       ccol[0:wdt, cchunk:cchunk + 1],
                                       None, ALU.is_equal))
                p.bank_done(cm_last, bank)
                out[(sh, sc)] = cm_last
            return out

        cm_blocks = {}
        pcol_toks = []
        for cchunk in range(CC):
            tt = p.op("pe", ps_row_ev + [t_onesb] + p.bank_dep(3),
                      lambda t, cchunk=cchunk: t.matmul(
                          pb[3][:, 0:1],
                          prob_row[0:1, cchunk * 128:(cchunk + 1) * 128],
                          ones_b[0:1, 0:1], start=True, stop=True))
            ev = p.op("dv", [tt], lambda v, cchunk=cchunk: v.tensor_copy(
                scr[:, 20 + cchunk:21 + cchunk], pb[3][:, 0:1]))
            p.bank_done(ev, 3)
            pcol_toks.append(ev)

        g_last = None
        h_pe_last = None
        for g in range(FG):
            cur = (e * FG + g) % 2
            tw1, tw3 = wg_tok[cur]
            w1g, w3g = wgbuf[cur]
            for fi in range(4):
                fc = g * 4 + fi
                par = fc % 2
                b1 = 0 if par == 0 else 4
                b3 = 2 if par == 0 else 6
                h1_last = None
                h3_last = None
                for j in range(MC):
                    for cs, (c0, cw) in enumerate(CS):
                        dep = [tw1]
                        if j == 0:
                            dep += p.bank_dep(b1 + cs)
                            if cs == 0:
                                dep += [disp_ev_last]
                        h1_last = p.op("pe", dep,
                                       lambda t, j=j, fi=fi, c0=c0, cw=cw, b1=b1,
                                       cs=cs, w1g=w1g: t.matmul(
                                           pb[b1 + cs][:, 0:cw],
                                           w1g[:, j * 512 + fi * 128:j * 512 + (fi + 1) * 128],
                                           disp[:, j * CAP + c0:j * CAP + c0 + cw],
                                           start=(j == 0), stop=(j == MC - 1)))
                for j in range(MC):
                    for cs, (c0, cw) in enumerate(CS):
                        dep = [tw3]
                        if j == 0:
                            dep += p.bank_dep(b3 + cs)
                        h3_last = p.op("pe", dep,
                                       lambda t, j=j, fi=fi, c0=c0, cw=cw, b3=b3,
                                       cs=cs, w3g=w3g: t.matmul(
                                           pb[b3 + cs][:, 0:cw],
                                           w3g[:, j * 512 + fi * 128:j * 512 + (fi + 1) * 128],
                                           disp[:, j * CAP + c0:j * CAP + c0 + cw],
                                           start=(j == 0), stop=(j == MC - 1)))
                h_pe_last = h3_last
                sl = []
                for cs, (c0, cw) in enumerate(CS):
                    sl.append(p.op("ac", [h1_last, g_last], lambda s, c0=c0,
                                   cw=cw, b1=b1, cs=cs: s.activation(
                                       s1[:, c0:c0 + cw], pb[b1 + cs][:, 0:cw],
                                       AF.Silu)))
                for cs in range(len(CS)):
                    p.bank_done(sl[cs], b1 + cs)
                gt = None
                for cs, (c0, cw) in enumerate(CS):
                    gt = p.op("dv", [sl[cs], h3_last],
                              lambda v, fc=fc, c0=c0, cw=cw, b3=b3, cs=cs:
                              v.tensor_tensor(
                                  gT[:, fc * CAP + c0:fc * CAP + c0 + cw],
                                  s1[:, c0:c0 + cw], pb[b3 + cs][:, 0:cw],
                                  ALU.mult))
                    p.bank_done(gt, b3 + cs)
                g_last = gt
            wg_free[cur] = h3_last
            if g + 2 < FG:
                load_group(e, g + 2)

        lastbuf = (e * FG + FG - 1) % 2
        otherbuf = 1 - lastbuf
        cgroups = []
        ci = 0
        while ci < CC:
            cgroups.append(list(range(ci, min(ci + 3, CC))))
            ci += 3
        ring = [w2s, wgbuf[lastbuf][0], wgbuf[lastbuf][1],
                wgbuf[otherbuf][0], wgbuf[otherbuf][1]]
        ring_free = [w2s_war[0], wg_free[lastbuf], wg_free[lastbuf],
                     wg_free[otherbuf], wg_free[otherbuf]]
        ring_free[0] = de_free if ring_free[0] is None else ring_free[0]
        oe_ev_last = None
        w2_pe_last = None
        w2idx = 0
        for gi, cg in enumerate(cgroups):
            mm_last = {}
            for g in range(FG):
                slot = w2idx % 5
                w2idx += 1
                tok = p.dma("gp", f"q_w2{slot}", [ring_free[slot]] +
                            ([de_free] if slot == 0 else []),
                            lambda gp, e=e, g=g, dst=ring[slot]: gp.dma_start(
                                dst[:].rearrange("q (fj m) -> q fj m", fj=4),
                                w2_in.ap()[e, g * 512:(g + 1) * 512, :]
                                .rearrange("(fj q) m -> q fj m", q=128)))
                w2t = ring[slot]
                for fi in range(4):
                    fc = g * 4 + fi
                    for ti, ct_ in enumerate(cg):
                        wdt = min(128, CAP - ct_ * 128)
                        for mh, (m0, mw) in enumerate(MH):
                            bank = 2 * ti + mh if len(MH) > 1 else ti
                            dep = [tok]
                            if fc == 0:
                                dep += [g_last]
                                dep += p.bank_dep(bank)
                            mm = p.op("pe", dep,
                                      lambda t, fc=fc, fi=fi, ti=ti, ct_=ct_,
                                      wdt=wdt, m0=m0, mw=mw, bank=bank, w2t=w2t:
                                      t.matmul(
                                          pb[bank][:, 0:mw],
                                          gT[:, fc * CAP + ct_ * 128:fc * CAP + ct_ * 128 + wdt],
                                          w2t[:, fi * D + m0:fi * D + m0 + mw],
                                          start=(fc == 0), stop=(fc == FC - 1)))
                            mm_last[(ti, mh)] = mm
                            w2_pe_last = mm
                ring_free[slot] = w2_pe_last
                if slot == 0:
                    w2s_war[0] = w2_pe_last
                if gi == len(cgroups) - 1 and g == 1:
                    cm_blocks.update(emit_cm_half(0))
            for ti, ct_ in enumerate(cg):
                wdt = min(128, CAP - ct_ * 128)
                for mh, (m0, mw) in enumerate(MH):
                    bank = 2 * ti + mh if len(MH) > 1 else ti
                    e1 = p.op("dv", [mm_last[(ti, mh)], pcol_toks[ct_]] +
                              ([h_pe_last] if gi == 0 and ti == 0 and mh == 0 else []),
                              lambda v, ct_=ct_, wdt=wdt, m0=m0, mw=mw, bank=bank:
                              v.tensor_scalar(
                                  out_e[0:wdt, ct_ * D + m0:ct_ * D + m0 + mw],
                                  pb[bank][0:wdt, 0:mw],
                                  scr[0:wdt, 20 + ct_:21 + ct_], None, ALU.mult))
                    e2 = p.op("dv", [e1] + bc_gb,
                              lambda v, ct_=ct_, wdt=wdt, m0=m0, mw=mw:
                              v.tensor_tensor(
                                  out_e[0:wdt, ct_ * D + m0:ct_ * D + m0 + mw],
                                  out_e[0:wdt, ct_ * D + m0:ct_ * D + m0 + mw],
                                  Mb[0:wdt, 5 * D + m0:5 * D + m0 + mw],
                                  ALU.mult))
                    p.bank_done(e2, bank)
                    oe_ev_last = e2
        if e + 1 < E:
            slot_of = {lastbuf: (1, 2), otherbuf: (3, 4)}
            b0 = ((e + 1) * FG) % 2
            wg_free[b0] = ring_free[slot_of[b0][0]]
            load_group(e + 1, 0, extra=[ring_free[slot_of[b0][1]]])
            if FG > 1:
                b1 = ((e + 1) * FG + 1) % 2
                wg_free[b1] = ring_free[slot_of[b1][0]]
                load_group(e + 1, 1, extra=[ring_free[slot_of[b1][1]]])

        for sh in range(SH):
            shs = S // SH
            if sh > 0:
                cm_blocks.update(emit_cm_half(sh))
            for i in range(sh * (NT // SH), (sh + 1) * (NT // SH)):
                i_in = i - sh * (NT // SH)
                bank = 4 + 2 * (i % 2) if len(MH) > 1 else 4 + (i % 2)
                last = None
                for cchunk in range(CC):
                    wdt = min(128, CAP - cchunk * 128)
                    for mh, (m0, mw) in enumerate(MH):
                        dep = []
                        if cchunk == 0 and mh == 0:
                            dep = [cm_blocks[(sh, i_in * 128 // 512)]]
                            for mh2 in range(len(MH)):
                                dep += p.bank_dep(bank + mh2)
                        last = p.op("pe", dep,
                                    lambda t, i_in=i_in, cchunk=cchunk, m0=m0,
                                    mw=mw, mh=mh, wdt=wdt, bank=bank: t.matmul(
                                        pb[bank + mh][:, 0:mw],
                                        CM[0:wdt, cchunk * shs + i_in * 128:
                                           cchunk * shs + (i_in + 1) * 128],
                                        out_e[0:wdt, cchunk * D + m0:
                                              cchunk * D + m0 + mw],
                                        start=(cchunk == 0), stop=(cchunk == CC - 1)))
                combine_pe_last = last
                for mh, (m0, mw) in enumerate(MH):
                    ta_ = p.op("dv", [last, out_add[i]],
                               lambda v, i=i, m0=m0, mw=mw, mh=mh, bank=bank:
                               v.tensor_tensor(
                                   x1[:, i * D + m0:i * D + m0 + mw],
                                   x1[:, i * D + m0:i * D + m0 + mw],
                                   pb[bank + mh][:, 0:mw], ALU.add))
                    p.bank_done(ta_, bank + mh)
                    out_add[i] = ta_

    out_toks = []
    for b2 in range(NT // 2):
        deps = [out_add[b2 * 2 + i_] for i_ in range(2)]
        out_toks.append(p.dma("sp", "q_x" if b2 % 2 == 0 else "q_misc", deps,
                              lambda s, b2=b2: s.dma_start(
                                  out_d.ap()[b2 * 256:(b2 + 1) * 256, :]
                                  .rearrange("(a q) d -> q a d", q=128),
                                  x1[:, b2 * 2 * D:(b2 + 1) * 2 * D])))
    p._emit("sp", out_toks, lambda s: None, None, 0)

    with nc.Block() as blk:
        p.realize(blk)
    nc.finalize()
    return nc


def kernel(L0_attn_x, L0_attn_Wk, L0_attn_Wq, L0_attn_Wv, L0_ffn_gate_w,
           L0_ffn_w1, L0_ffn_w2, L0_ffn_w3, L0_c, L0_wmod):
    B = L0_attn_x.shape[0]
    nc = build_nc()
    in_maps = []
    for b in range(B):
        in_maps.append({
            "x": np.ascontiguousarray(np.asarray(L0_attn_x)[b]),
            "c": np.ascontiguousarray(np.asarray(L0_c)[b:b + 1]),
            "wmod": np.asarray(L0_wmod),
            "wk": np.asarray(L0_attn_Wk),
            "wq": np.asarray(L0_attn_Wq),
            "wv": np.asarray(L0_attn_Wv),
            "gw": np.asarray(L0_ffn_gate_w),
            "w1": np.asarray(L0_ffn_w1),
            "w2": np.asarray(L0_ffn_w2),
            "w3": np.asarray(L0_ffn_w3),
        })
    res = run_bass_kernel_spmd(nc, in_maps, core_ids=list(range(B)))
    return np.stack([np.asarray(r["out"]) for r in res.results], axis=0)
